# version 06
# baseline (speedup 1.0000x reference)
import os
os.environ.setdefault("TRNINF_ENABLE_CUSTOMCOMMS_RDH_AG", "1")
os.environ.setdefault("TRNINF_ENABLE_CUSTOMCOMMS_RDH_AR", "1")

import math
import numpy as np

import concourse.bass as bass
import concourse.mybir as mybir
from concourse.bass_utils import run_bass_kernel_spmd

F32 = mybir.dt.float32
U32 = mybir.dt.uint32
ALU = mybir.AluOpType
AF = mybir.ActivationFunctionType
AX = mybir.AxisListType

NCORE = 8
B, T, C = 16, 2048, 1024
H, HD, RD = 16, 64, 32
KVR, QR = 256, 512
BPC = B // NCORE
JPC = T // NCORE
FROWS = B * JPC
FH, FDH = 16, 64
SC_ATT = 1.0 / math.sqrt(HD + RD)
SC_FFN = 1.0 / math.sqrt(FDH)

ENGINES = ("sync", "tensor", "vector", "scalar", "gpsimd")


class Sched:
    def __init__(self, nc):
        self.nc = nc
        self.ops = {e: [] for e in ENGINES}
        self.sem = {e: nc.alloc_semaphore(f"sem_{e}") for e in ENGINES}
        self.cnt = {e: 0 for e in ENGINES}
        self.dsem = {e: nc.alloc_semaphore(f"dsem_{e}") for e in ("sync", "gpsimd")}
        self.dcnt = {e: 0 for e in ("sync", "gpsimd")}
        self.ccsem = nc.alloc_semaphore("sem_cc")
        self.cccnt = 0

    def emit(self, eng, fn, tok=False):
        if tok:
            self.cnt[eng] += 1
            sem, val = self.sem[eng], self.cnt[eng]
            def wrapped(e, fn=fn, sem=sem):
                fn(e).then_inc(sem, 1)
            self.ops[eng].append(wrapped)
            return (eng, "c", val)
        self.ops[eng].append(fn)
        return None

    def tok(self, eng):
        return (eng, "c", self.cnt[eng])

    def dtok(self, eng):
        return (eng, "d", self.dcnt[eng])

    def dma(self, eng, out, in_, tok=True, **kw):
        if tok:
            self.dcnt[eng] += 16
            sem, val = self.dsem[eng], self.dcnt[eng]
            def wrapped(e, out=out, in_=in_, sem=sem, kw=kw):
                e.dma_start(out=out, in_=in_, **kw).then_inc(sem, 16)
            self.ops[eng].append(wrapped)
            return (eng, "d", val)
        def plain(e, out=out, in_=in_, kw=kw):
            e.dma_start(out=out, in_=in_, **kw)
        self.ops[eng].append(plain)
        return None

    def idma(self, out, in_, idx_ap):
        self.dcnt["gpsimd"] += 16
        sem, val = self.dsem["gpsimd"], self.dcnt["gpsimd"]
        def wrapped(e, out=out, in_=in_, idx_ap=idx_ap, sem=sem):
            e.indirect_dma_start(
                out=out, out_offset=None, in_=in_,
                in_offset=bass.IndirectOffsetOnAxis(ap=idx_ap, axis=0),
            ).then_inc(sem, 16)
        self.ops["gpsimd"].append(wrapped)
        return ("gpsimd", "d", val)

    def collective(self, kind, op, ins, outs):
        self.cccnt += 1
        val = self.cccnt
        sem = self.ccsem
        def fn(e, kind=kind, op=op, ins=ins, outs=outs, sem=sem):
            e.collective_compute(
                kind, op, replica_groups=[list(range(NCORE))],
                ins=ins, outs=outs,
            ).then_inc(sem, 1)
        self.ops["gpsimd"].append(fn)
        return ("cc", "c", val)

    def wait(self, eng, tk):
        if tk is None:
            return
        src, kind, val = tk
        sem = self.ccsem if src == "cc" else (self.dsem[src] if kind == "d" else self.sem[src])
        self.ops[eng].append(lambda e, sem=sem, val=val: e.wait_ge(sem, val))

    def waits(self, eng, toks):
        best = {}
        for t in toks:
            if t is None:
                continue
            k = (t[0], t[1])
            if k not in best or best[k][2] < t[2]:
                best[k] = t
        for t in best.values():
            self.wait(eng, t)


class Arena:
    def __init__(self, nc, words):
        self.t = nc.alloc_sbuf_tensor("arena", [128, words], F32).ap()
        self.words = words
        self.off = 0
        self.marks = []

    def alloc(self, words, align=128):
        off = -(-self.off // align) * align
        assert off + words <= self.words, f"arena overflow {off}+{words}>{self.words}"
        self.off = off + words
        return self.t[:, off:off + words]

    def push(self):
        self.marks.append(self.off)

    def pop(self):
        self.off = self.marks.pop()


def host_constants():
    c = {}
    c["ident"] = np.eye(128, dtype=np.float32)
    g = np.arange(896)[None, :]
    p = np.arange(128)[:, None]
    c["mstrip"] = np.where(g < p + 384, -1e30, 0.0).astype(np.float32)
    inv_freq = 1.0 / 10000.0 ** (np.arange(0, RD, 2, dtype=np.float32) / RD)
    t = np.arange(T, dtype=np.float32)
    freqs = np.outer(t, inv_freq)
    emb = np.concatenate([freqs, freqs], -1)
    c["coss"] = np.tile(np.cos(emb).T, (4, 1)).astype(np.float32)
    c["sins"] = np.tile(np.sin(emb).T, (4, 1)).astype(np.float32)
    P = np.zeros((128, 128), dtype=np.float32)
    for blk in range(4):
        b0 = blk * 32
        for d in range(16):
            P[b0 + d + 16, b0 + d] = -1.0
            P[b0 + d, b0 + d + 16] = 1.0
    c["pmat"] = P
    repj = np.zeros((8, 128), dtype=np.float32)
    for j in range(8):
        repj[j, j * 16:(j + 1) * 16] = 1.0
    c["repj"] = repj
    reg = np.zeros((16, 128), dtype=np.float32)
    for j in range(8):
        for g in range(16):
            reg[g, j * 16 + g] = 1.0
    c["reg"] = reg
    mj = np.zeros((128, 128), dtype=np.float32)
    for j in range(8):
        mj[j * 16:(j + 1) * 16, j * 16:(j + 1) * 16] = 1.0
    c["mj"] = mj
    c["ones_1x64"] = np.ones((1, 64), dtype=np.float32)
    c["ones_1x128"] = np.ones((1, 128), dtype=np.float32)
    c["ones_128x1"] = np.ones((128, 1), dtype=np.float32)
    return c


def host_ridx(rank):
    idx = np.zeros((128, 32), dtype=np.uint32)
    for t in range(32):
        for jr in range(8):
            for i in range(16):
                idx[jr * 16 + i, t] = i * T + rank * JPC + 8 * t + jr
    return idx


def _r(ap, spec, **kw):
    return ap.rearrange(spec, **kw)


def _wview(hbm_ap, a, b):
    return _r(hbm_ap, "(a b) c -> b a c", a=a, b=b)


def _sview(sb_ap, a, c):
    return _r(sb_ap, "p (a c) -> p a c", a=a, c=c)


def build_program(upto="all", taps=()):
    taps = set(taps)
    nc = bass.Bass("TRN2", target_bir_lowering=False)

    def din(name, shape, dt=F32):
        return nc.dram_tensor(name, list(shape), dt, kind="ExternalInput").ap()

    io = {}
    io["xa"] = din("xa", (BPC, T, C))
    io["xf"] = din("xf", (FROWS, C))
    for n, sh in [("wkv", (C, KVR)), ("gkv", (KVR,)), ("wkdec", (KVR, C)),
                  ("wvdec", (KVR, C)), ("wq", (C, QR)), ("wqdec", (QR, C)),
                  ("wkrope", (C, H * RD)), ("wqrope", (QR, H * RD)), ("wo", (C, C)),
                  ("fwg", (C, FH)), ("fwq", (FH, C, FDH)), ("fbqT", (FDH, FH)),
                  ("fwkv", (FH, C, 2 * FDH)), ("fbkvT", (2 * FDH, FH)),
                  ("fwo", (FH, FDH, C)), ("fbo", (C,)), ("flng", (C,)), ("flnb", (C,)),
                  ("ident", (128, 128)), ("mstrip", (128, 896)), ("coss", (128, T)),
                  ("sins", (128, T)), ("pmat", (128, 128)), ("ones_1x64", (1, 64)),
                  ("ones_1x128", (1, 128)), ("ones_128x1", (128, 1)),
                  ("repj", (8, 128)), ("reg", (16, 128)), ("mj", (128, 128))]:
        io[n] = din(n, sh)
    io["ridx"] = din("ridx", (128, 32), U32)

    out = nc.dram_tensor("out", [B, JPC, C], F32, kind="ExternalOutput").ap()

    h_scr = nc.dram_tensor("h_scr", [FROWS, C], F32).ap()
    ssc_scr = nc.dram_tensor("ssc_scr", [32, 128, 128], F32).ap()
    kro_scr = nc.dram_tensor("kro_scr", [H * RD, T], F32).ap()
    qro_scr = nc.dram_tensor("qro_scr", [H * RD, T], F32).ap()
    oa_scr = nc.dram_tensor("oa_scr", [C, T], F32).ap()
    ao_scr = nc.dram_tensor("ao_scr", [FROWS, C], F32).ap()
    p_scr = nc.dram_tensor("p_scr", [FH, JPC], F32).ap()
    dbg_y = nc.dram_tensor("dbg_y", [FROWS, C], F32).ap()
    dbg_f = nc.dram_tensor("dbg_f", [FROWS, C], F32).ap()
    cc_att_in = nc.dram_tensor("cc_att_in", [1, FH * JPC], F32)
    cc_att_out = nc.dram_tensor("cc_att_out", [NCORE, FH * JPC], F32)
    cc_ra_in = nc.dram_tensor("cc_ra_in", [BPC * T, C], F32)
    cc_ra_out = nc.dram_tensor("cc_ra_out", [NCORE * BPC * T, C], F32)

    tap_aps = {}
    def tap(name, shape):
        if name in taps:
            tap_aps[name] = nc.dram_tensor("tap_" + name, list(shape), F32,
                                           kind="ExternalOutput").ap()
        return tap_aps.get(name)

    s = Sched(nc)
    A = Arena(nc, 50944)

    ident = A.alloc(128)
    mstrip = A.alloc(896)
    pmat = A.alloc(128)
    coss = A.alloc(T)
    sins = A.alloc(T)
    o1x64 = A.alloc(64)[0:1, :]
    o1x128 = A.alloc(128)[0:1, :]
    o128x1 = A.alloc(1)
    gkv_sb = A.alloc(2)
    fbqT_sb = A.alloc(FH)
    fbkvT_sb = A.alloc(FH)
    lngb = A.alloc(C)
    lnbb = A.alloc(C)
    bob = A.alloc(C)
    eps_ln = A.alloc(1)
    eps_rms = A.alloc(1)
    gap1 = A.alloc(8)
    repj_sb = A.alloc(128)[0:8, :]
    reg_sb = A.alloc(128)[0:16, :]
    mj_sb = A.alloc(128)
    ridx_sb = A.alloc(32).bitcast(U32)
    vall = A.alloc(FH * JPC)

    psS = nc.alloc_psum_tensor("psS", [128, 2048], F32).ap()
    psPV = nc.alloc_psum_tensor("psPV", [128, 512], F32).ap()
    psA = nc.alloc_psum_tensor("psA", [128, 512], F32).ap()
    psB = nc.alloc_psum_tensor("psB", [128, 512], F32).ap()
    psT = nc.alloc_psum_tensor("psT", [128, 512], F32).ap()

    ctoks = [
        s.dma("sync", ident, io["ident"]),
        s.dma("sync", mstrip, io["mstrip"]),
        s.dma("sync", pmat, io["pmat"]),
        s.dma("sync", coss, io["coss"]),
        s.dma("sync", sins, io["sins"]),
        s.dma("sync", o1x64, io["ones_1x64"]),
        s.dma("sync", o1x128, io["ones_1x128"]),
        s.dma("sync", o128x1, io["ones_128x1"]),
        s.dma("sync", gkv_sb, _r(io["gkv"], "(a b) -> b a", a=2, b=128),
              allow_slow_non_contiguous=True),
        s.dma("sync", fbqT_sb[0:FDH, :], io["fbqT"]),
        s.dma("sync", fbkvT_sb, io["fbkvT"]),
        s.dma("sync", ridx_sb, io["ridx"]),
        s.dma("sync", repj_sb, io["repj"]),
        s.dma("sync", reg_sb, io["reg"]),
        s.dma("sync", mj_sb, io["mj"]),
    ]

    A.push()
    row3 = A.alloc(3 * C)
    r_flng, r_flnb, r_fbo = row3[0:1, 0:C], row3[0:1, C:2 * C], row3[0:1, 2 * C:3 * C]
    ctoks.append(s.dma("sync", r_flng, io["flng"].unsqueeze(0)))
    ctoks.append(s.dma("sync", r_flnb, io["flnb"].unsqueeze(0)))
    ctoks.append(s.dma("sync", r_fbo, io["fbo"].unsqueeze(0)))
    s.waits("tensor", ctoks)
    for row, dst in ((r_flng, lngb), (r_flnb, lnbb), (r_fbo, bob)):
        for hf in range(2):
            tmm = s.emit("tensor", lambda e, row=row, hf=hf: e.matmul(
                psB, o1x128, row[:, hf * 512:(hf + 1) * 512], start=True, stop=True),
                tok=True)
            s.wait("vector", tmm)
            tv = s.emit("vector", lambda e, dst=dst, hf=hf: e.tensor_copy(
                dst[:, hf * 512:(hf + 1) * 512], psB), tok=True)
            s.wait("tensor", tv)
    s.emit("vector", lambda e: e.memset(eps_ln, 1e-5))
    s.emit("vector", lambda e: e.memset(eps_rms, 1e-6))
    s.emit("vector", lambda e: e.memset(gap1, 0.0))
    s.wait("sync", s.tok("tensor"))
    A.pop()

    def gapv():
        s.emit("vector", lambda e: e.tensor_copy(gap1[:, 0:4], gap1[:, 4:8]))

    def rsqrt_chain(v_ap, r_ap, t0_ap, t1_ap, pre_tok):
        s.wait("scalar", pre_tok)
        s.emit("scalar", lambda e: e.activation(out=r_ap, in_=v_ap, func=AF.Ln))
        ta = s.emit("scalar", lambda e: e.activation(out=r_ap, in_=r_ap,
                                                     func=AF.Exp, scale=-0.5), tok=True)
        s.wait("gpsimd", ta)
        s.emit("gpsimd", lambda e: e.tensor_tensor(t0_ap, r_ap, r_ap, ALU.mult))
        s.emit("gpsimd", lambda e: e.tensor_tensor(t1_ap, v_ap, t0_ap, ALU.mult))
        s.emit("gpsimd", lambda e: e.tensor_scalar(t1_ap, t1_ap, -0.5, 1.5,
                                                   ALU.mult, op1=ALU.add))
        return s.emit("gpsimd", lambda e: e.tensor_tensor(r_ap, r_ap, t1_ap,
                                                          ALU.mult), tok=True)

    def layernorm(xin, xout, st, D, eps_tile):
        s.emit("vector", lambda e: e.tensor_scalar(st["sq"], xin, 1.0, 0.0,
                                                   ALU.mult, op1=ALU.add,
                                                   accum_out=st["m"]))
        s.emit("vector", lambda e: e.tensor_tensor(st["sq"], xin, xin, ALU.mult))
        tv = s.emit("vector", lambda e: e.tensor_scalar(st["sq"], st["sq"], 1.0, 0.0,
                                                        ALU.mult, op1=ALU.add,
                                                        accum_out=st["ss"]), tok=True)
        s.wait("gpsimd", tv)
        s.emit("gpsimd", lambda e: e.tensor_scalar(st["m"], st["m"], 1.0 / D, None, ALU.mult))
        s.emit("gpsimd", lambda e: e.tensor_scalar(st["v"], st["ss"], 1.0 / D, None, ALU.mult))
        s.emit("gpsimd", lambda e: e.tensor_tensor(st["t0"], st["m"], st["m"], ALU.mult))
        s.emit("gpsimd", lambda e: e.tensor_tensor(st["v"], st["v"], st["t0"], ALU.subtract))
        tg = s.emit("gpsimd", lambda e: e.tensor_scalar(st["v"], st["v"], eps_tile,
                                                        None, ALU.add), tok=True)
        tn = rsqrt_chain(st["v"], st["r"], st["t0"], st["t1"], tg)
        s.wait("vector", tn)
        return s.emit("vector", lambda e: e.tensor_scalar(xout, xin, st["m"], st["r"],
                                                          ALU.subtract, op1=ALU.mult),
                      tok=True)

    def transpose(dst_psum, src, tok=False):
        p = src.partition_size()
        idn = ident if p == 128 else ident[0:p, 0:p]
        return s.emit("tensor", lambda e, dst_psum=dst_psum, src=src, idn=idn:
                      e.transpose(dst_psum, src, idn), tok=tok)

    A.push()
    xr = [A.alloc(C), A.alloc(C)]
    hrow = [A.alloc(C), A.alloc(C)]
    st1 = {k: A.alloc(1) for k in ("m", "ss", "v", "r", "t0", "t1")}
    st1["sq"] = A.alloc(C)
    hfm = A.alloc(8 * 128)
    lfm = A.alloc(128)[0:16, :]
    ltm = A.alloc(16)
    nmx = A.alloc(1)
    zsum = A.alloc(1)
    ziv = A.alloc(1)
    probs = A.alloc(16)
    prT = A.alloc(128)[0:16, :]
    smax = A.alloc(8)[0:16, :]
    smaxT = A.alloc(16)[0:8, :]
    scol = A.alloc(16)
    scolT = A.alloc(128)[0:16, :]
    sgf = A.alloc(32 * 128)
    sst = A.alloc(128)
    hg = A.alloc(8 * 2048)
    gst = A.alloc(1024)
    fwq_sb = [A.alloc(8 * FDH), A.alloc(8 * FDH)]
    fwkv_sb = [A.alloc(8 * 2 * FDH), A.alloc(8 * 2 * FDH)]
    fwg_sb = A.alloc(8 * FH)
    qg = A.alloc(JPC)
    kg = A.alloc(JPC)
    kvst = A.alloc(JPC)
    dotst = A.alloc(JPC)[0:1, :]

    tk_fwg = s.dma("sync", _sview(fwg_sb, 8, FH), _wview(io["fwg"], 8, 128))
    hg_view = _r(hg, "p (c g j) -> p c g j", c=8, g=8, j=JPC)
    guards = {}

    def p1_tile(t, pass2):
        buf = t % 2
        if not pass2:
            s.waits("sync", [guards.get(("xr", buf))])
            tk_x = s.dma("sync", xr[buf], io["xf"][t * 128:(t + 1) * 128, :])
            s.waits("vector", [tk_x, guards.get(("hrow", buf))])
            tk_h = layernorm(xr[buf], hrow[buf], st1, C, eps_ln)
            guards[("xr", buf)] = tk_h
            s.wait("gpsimd", tk_h)
            s.dma("gpsimd", h_scr[t * 128:(t + 1) * 128, :], hrow[buf], tok=True)
            s.wait("tensor", tk_h)
            for ch in range(8):
                pb = psT if ch % 2 == 0 else psB
                tt = transpose(pb[:, 0:128], hrow[buf][:, ch * 128:(ch + 1) * 128],
                               tok=True)
                s.wait("vector", tt)
                tv = s.emit("vector", lambda e, ch=ch, pb=pb: e.tensor_copy(
                    hfm[:, ch * 128:(ch + 1) * 128], pb[:, 0:128]), tok=True)
                s.wait("tensor", tv)
            s.waits("tensor", [tk_fwg])
            for ch in range(8):
                s.emit("tensor", lambda e, ch=ch: e.matmul(
                    psA[0:16, 0:128], fwg_sb[:, ch * 16:(ch + 1) * 16],
                    hfm[:, ch * 128:(ch + 1) * 128], start=(ch == 0), stop=(ch == 7)),
                    tok=(ch == 7))
            tmm = s.tok("tensor")
            s.wait("vector", tmm)
            tv = s.emit("vector", lambda e: e.tensor_copy(lfm, psA[0:16, 0:128]), tok=True)
            s.wait("tensor", tv)
            tt = transpose(psT[0:128, 0:16], lfm, tok=True)
            s.wait("vector", tt)
            s.emit("vector", lambda e: e.tensor_copy(ltm, psT[:, 0:16]))
            gapv()
            s.emit("vector", lambda e: e.reduce_max(nmx, ltm, axis=AX.X, negate=True))
            gapv()
            tv2 = s.emit("vector", lambda e: e.tensor_copy(gap1[:, 0:4], gap1[:, 4:8]),
                         tok=True)
            s.wait("scalar", tv2)
            ta = s.emit("scalar", lambda e: e.activation(out=probs, in_=ltm, func=AF.Exp,
                                                         bias=nmx, accum_out=zsum),
                        tok=True)
            s.wait("vector", ta)
            s.emit("vector", lambda e: e.reciprocal(ziv, zsum))
            gapv()
            tv = s.emit("vector", lambda e: e.tensor_scalar(probs, probs, ziv, None,
                                                            ALU.mult), tok=True)
            s.wait("tensor", tv)
            tt = transpose(psT[0:16, 0:128], probs, tok=True)
            s.wait("vector", tt)
            tv = s.emit("vector", lambda e: e.tensor_copy(prT, psT[0:16, 0:128]),
                        tok=True)
            gapv()
            prv = _r(prT, "p (j i) -> p j i", j=8, i=16)
            tv2 = s.emit("vector", lambda e, prv=prv: e.reduce_max(smax, prv, axis=AX.X),
                         tok=True)
            s.wait("tensor", tv2)
            tt = transpose(psB[0:8, 0:16], smax, tok=True)
            s.wait("vector", tt)
            tv = s.emit("vector", lambda e: e.tensor_copy(smaxT, psB[0:8, 0:16]),
                        tok=True)
            s.wait("tensor", tv)
            tmm = s.emit("tensor", lambda e: e.matmul(psB[:, 0:16], repj_sb, smaxT,
                                                      start=True, stop=True), tok=True)
            s.wait("vector", tmm)
            tv = s.emit("vector", lambda e: e.tensor_tensor(scol, probs, psB[:, 0:16],
                                                            ALU.is_equal), tok=True)
            s.wait("tensor", tv)
            tt = transpose(psB[0:16, 0:128], scol, tok=True)
            s.wait("vector", tt)
            tv = s.emit("vector", lambda e: e.tensor_copy(scolT, psB[0:16, 0:128]),
                        tok=True)
            s.wait("tensor", tv)
            tmm = s.emit("tensor", lambda e: e.matmul(psB[:, 0:128], reg_sb, scolT,
                                                      start=True, stop=True), tok=True)
            s.wait("vector", tmm)
            tv = s.emit("vector", lambda e: e.tensor_tensor(sst, psB[:, 0:128], mj_sb, ALU.mult),
                        tok=True)
            s.wait("gpsimd", tv)
            s.dma("gpsimd", ssc_scr[t], sst, tok=True)
            s.wait("tensor", tv)
            tt = transpose(psB[:, 0:128], sst, tok=True)
            s.wait("vector", tt)
            tsg = s.emit("vector", lambda e, t=t: e.tensor_copy(
                sgf[:, t * 128:(t + 1) * 128], psB[:, 0:128]), tok=True)
            s.wait("tensor", tsg)
            hf0 = 0
        else:
            s.waits("sync", [guards.get(("hrow", buf))])
            tk_x = s.dma("sync", hrow[buf], h_scr[t * 128:(t + 1) * 128, :])
            s.wait("tensor", tk_x)
            hf0 = 1
        for cf in range(2):
            s.emit("tensor", lambda e, t=t, cf=cf, buf=buf: e.matmul(
                psA if cf == 0 else psB,
                sgf[:, t * 128:(t + 1) * 128],
                hrow[buf][:, cf * 512:(cf + 1) * 512], start=True, stop=True),
                tok=(cf == 1))
        tmm = s.tok("tensor")
        guards[("hrow", buf)] = tmm
        s.wait("vector", tmm)
        s.emit("vector", lambda e: e.tensor_copy(gst[:, 0:512], psA))
        tv = s.emit("vector", lambda e: e.tensor_copy(gst[:, 512:1024], psB),
                    tok=True)
        s.wait("tensor", tv)
        for ch in range(8):
            tt = transpose(psT[0:128, 0:128], gst[:, ch * 128:(ch + 1) * 128], tok=True)
            s.wait("vector", tt)
            srcv = _r(psT[:, 0:128], "p (j g) -> p g j", j=8, g=16)[:, hf0 * 8:hf0 * 8 + 8, :]
            dst = hg_view[:, ch, :, 8 * t: 8 * t + 8]
            tv = s.emit("vector", lambda e, srcv=srcv, dst=dst: e.tensor_copy(dst, srcv),
                        tok=True)
            s.wait("tensor", tv)

    def p1_per_g(half):
        prev_vdma = None
        prev_ddma = None
        prev_dot = None
        for gl in range(8):
            g = 8 * half + gl
            pb = gl % 2
            s.waits("sync", [guards.get(("fw", pb))])
            tkw1 = s.dma("sync", _sview(fwq_sb[pb], 8, FDH),
                         _wview(io["fwq"][g], 8, 128))
            tkw2 = s.dma("sync", _sview(fwkv_sb[pb], 8, 2 * FDH),
                         _wview(io["fwkv"][g], 8, 128))
            s.waits("tensor", [tkw1, tkw2])
            for ch in range(8):
                s.emit("tensor", lambda e, ch=ch, pb=pb, gl=gl: e.matmul(
                    psA[0:64, 0:256], fwq_sb[pb][:, ch * 64:(ch + 1) * 64],
                    hg_view[:, ch, gl, :], start=(ch == 0), stop=(ch == 7)),
                    tok=(ch == 7))
            tq = s.tok("tensor")
            for ch in range(8):
                s.emit("tensor", lambda e, ch=ch, pb=pb, gl=gl: e.matmul(
                    psB[:, 0:256], fwkv_sb[pb][:, ch * 128:(ch + 1) * 128],
                    hg_view[:, ch, gl, :], start=(ch == 0), stop=(ch == 7)),
                    tok=(ch == 7))
            tkv = s.tok("tensor")
            guards[("fw", pb)] = tkv
            s.waits("scalar", [tq, prev_dot])
            ta1 = s.emit("scalar", lambda e, g=g: e.activation(
                out=qg[0:64, :], in_=psA[0:64, 0:256], func=AF.Identity,
                bias=fbqT_sb[0:64, g:g + 1]), tok=True)
            s.waits("scalar", [tkv, prev_vdma])
            ta2 = s.emit("scalar", lambda e, g=g: e.activation(
                out=kg[0:64, :], in_=psB[0:64, 0:256], func=AF.Identity,
                bias=fbkvT_sb[0:64, g:g + 1]), tok=True)
            ta3 = s.emit("scalar", lambda e, g=g: e.activation(
                out=kvst[64:128, :], in_=psB[64:128, 0:256], func=AF.Identity,
                bias=fbkvT_sb[64:128, g:g + 1]), tok=True)
            s.wait("gpsimd", ta3)
            prev_vdma = s.dma("gpsimd", vall[0:64, g * 256:(g + 1) * 256],
                              kvst[64:128, :], tok=True)
            s.waits("vector", [ta1, ta2])
            tv = s.emit("vector", lambda e: e.tensor_tensor(kvst[0:64, :], qg[0:64, :],
                                                            kg[0:64, :], ALU.mult),
                        tok=True)
            s.wait("tensor", tv)
            tmm = s.emit("tensor", lambda e: e.matmul(
                psT[0:1, 0:256], o128x1[0:64, :], kvst[0:64, :], start=True, stop=True),
                tok=True)
            s.waits("vector", [tmm, prev_ddma])
            prev_dot = s.emit("vector", lambda e: e.tensor_copy(dotst[:, 0:256],
                                                                psT[0:1, 0:256]), tok=True)
            s.wait("gpsimd", prev_dot)
            prev_ddma = s.dma("gpsimd", cc_att_in.ap()[:, g * 256:(g + 1) * 256],
                              dotst[:, 0:256], tok=True)

    tk_ccatt = None
    if upto in ("p1", "p2", "all"):
        for t in range(32):
            p1_tile(t, pass2=False)
        p1_per_g(0)
        s.wait("sync", s.tok("tensor"))
        for t in range(32):
            p1_tile(t, pass2=True)
        p1_per_g(1)
        s.wait("gpsimd", s.dtok("gpsimd"))
        tk_ccatt = s.collective("AllGather", ALU.bypass,
                                [cc_att_in.ap().opt()], [cc_att_out.ap().opt()])
        if tap("hg", (128, 8 * 2048)) is not None:
            tv = s.tok("vector")
            s.wait("gpsimd", tv)
            s.dma("gpsimd", tap_aps["hg"], hg, tok=True)
        if tap("vall", (128, FH * JPC)) is not None:
            s.dma("gpsimd", tap_aps["vall"], vall, tok=True)
        if tap("attin", (1, FH * JPC)) is not None:
            s.dma("gpsimd", tap_aps["attin"], cc_att_in.ap(), tok=True)
        if tap("hscr", (FROWS, C)) is not None:
            s.dma("gpsimd", tap_aps["hscr"], h_scr, tok=True)

    A.pop()

    A.push()
    st2 = {k: A.alloc(1) for k in ("m", "ss", "v", "r", "t0", "t1")}
    st2["sq"] = A.alloc(C)
    kvc = A.alloc(2 * T)
    qc = A.alloc(4 * T)
    A.push()
    sq2 = A.alloc(512)
    xr2 = [A.alloc(C), A.alloc(C)]
    xn2 = [A.alloc(C), A.alloc(C)]
    xnf = A.alloc(8 * 512)
    wkv_sb = A.alloc(8 * KVR)
    wq_sb = A.alloc(8 * QR)
    wro_sb = A.alloc(8 * QR)
    ro_raw = A.alloc(512)
    ro_out = A.alloc(512)
    rms_s = A.alloc(512)[0:1, :]
    rms_r = A.alloc(512)[0:1, :]
    rms_t0 = A.alloc(512)[0:1, :]
    rms_t1 = A.alloc(512)[0:1, :]
    A.pop()

    rope_state = {"prev": None}

    def rope_block(dst_scr, oc, g4, psrc):
        tv = s.emit("vector", lambda e, psrc=psrc: e.tensor_copy(ro_raw, psrc), tok=True)
        s.wait("tensor", tv)
        trot = s.emit("tensor", lambda e: e.matmul(psB, pmat, ro_raw,
                                                   start=True, stop=True), tok=True)
        tsl = slice(g4 * 512, (g4 + 1) * 512)
        s.waits("vector", [rope_state["prev"]])
        s.emit("vector", lambda e, tsl=tsl: e.tensor_tensor(
            ro_out, ro_raw, coss[:, tsl], ALU.mult))
        s.wait("vector", trot)
        s.emit("vector", lambda e, tsl=tsl: e.tensor_tensor(
            ro_raw, psB, sins[:, tsl], ALU.mult))
        gapv()
        tv = s.emit("vector", lambda e: e.tensor_tensor(ro_out, ro_out, ro_raw,
                                                        ALU.add), tok=True)
        s.wait("gpsimd", tv)
        tk = s.dma("gpsimd", dst_scr[oc * 128:(oc + 1) * 128, tsl], ro_out, tok=True)
        rope_state["prev"] = tk
        return tk

    g2 = {}

    def a_phase(b):
        s.wait("sync", s.tok("tensor"))
        tw1 = s.dma("sync", _sview(wkv_sb, 8, KVR), _wview(io["wkv"], 8, 128))
        tw2 = s.dma("sync", _sview(wq_sb, 8, QR), _wview(io["wq"], 8, 128))
        tw3 = s.dma("sync", _sview(wro_sb, 8, QR), _wview(io["wkrope"], 8, 128))
        rope_toks = []
        for g4 in range(4):
            for sub in range(4):
                buf = sub % 2
                s.waits("sync", [g2.get(("xr2", buf))])
                tk_x = s.dma("sync", xr2[buf],
                             io["xa"][b, g4 * 512 + sub * 128: g4 * 512 + (sub + 1) * 128, :])
                s.waits("vector", [tk_x, g2.get(("xn2", buf))])
                tk_n = layernorm(xr2[buf], xn2[buf], st2, C, eps_ln)
                g2[("xr2", buf)] = tk_n
                s.wait("tensor", tk_n)
                for ch in range(8):
                    pb = psT if ch % 2 == 0 else psB
                    tt = transpose(pb[:, 0:128], xn2[buf][:, ch * 128:(ch + 1) * 128],
                                   tok=True)
                    if ch == 7:
                        g2[("xn2", buf)] = tt
                    s.wait("vector", tt)
                    tv = s.emit("vector", lambda e, ch=ch, sub=sub, pb=pb: e.tensor_copy(
                        xnf[:, ch * 512 + sub * 128: ch * 512 + sub * 128 + 128],
                        pb[:, 0:128]), tok=True)
                    s.wait("tensor", tv)
            s.waits("tensor", [tw1, tw2, tw3])

            def proj_into(dst, w_sb, noc, wstride):
                for oc in range(noc):
                    pb = psA if oc % 2 == 0 else psT
                    for ch in range(8):
                        s.emit("tensor", lambda e, oc=oc, ch=ch, pb=pb, w_sb=w_sb,
                               wstride=wstride: e.matmul(
                            pb, w_sb[:, ch * wstride + oc * 128: ch * wstride + oc * 128 + 128],
                            xnf[:, ch * 512:(ch + 1) * 512],
                            start=(ch == 0), stop=(ch == 7)), tok=(ch == 7))
                    tmm = s.tok("tensor")
                    s.wait("vector", tmm)
                    tv = s.emit("vector", lambda e, dst=dst, oc=oc, pb=pb, g4=g4:
                                e.tensor_copy(
                                    dst[:, oc * T + g4 * 512: oc * T + g4 * 512 + 512], pb),
                                tok=True)
                    s.wait("tensor", tv)

            proj_into(kvc, wkv_sb, 2, KVR)
            proj_into(qc, wq_sb, 4, QR)
            for oc in range(4):
                for ch in range(8):
                    s.emit("tensor", lambda e, oc=oc, ch=ch: e.matmul(
                        psA, wro_sb[:, ch * QR + oc * 128: ch * QR + oc * 128 + 128],
                        xnf[:, ch * 512:(ch + 1) * 512],
                        start=(ch == 0), stop=(ch == 7)), tok=(ch == 7))
                s.wait("vector", s.tok("tensor"))
                rope_toks.append(rope_block(kro_scr, oc, g4, psA))
        s.wait("sync", s.tok("tensor"))
        tw4 = s.dma("sync", _sview(wro_sb[:, 0:4 * QR], 4, QR),
                    _wview(io["wqrope"], 4, 128))
        s.waits("tensor", [tw4])
        for g4 in range(4):
            for oc in range(4):
                for ch in range(4):
                    s.emit("tensor", lambda e, oc=oc, ch=ch, g4=g4: e.matmul(
                        psA, wro_sb[:, ch * QR + oc * 128: ch * QR + oc * 128 + 128],
                        qc[:, ch * T + g4 * 512: ch * T + g4 * 512 + 512],
                        start=(ch == 0), stop=(ch == 3)), tok=(ch == 3))
                s.wait("vector", s.tok("tensor"))
                rope_toks.append(rope_block(qro_scr, oc, g4, psA))
        for g4 in range(4):
            tsl = slice(g4 * 512, (g4 + 1) * 512)
            for ch in range(2):
                tvq = s.emit("vector", lambda e, ch=ch, tsl=tsl: e.tensor_tensor(
                    sq2, kvc[:, ch * T:][:, tsl], kvc[:, ch * T:][:, tsl], ALU.mult),
                    tok=True)
                s.wait("tensor", tvq)
                tms = s.emit("tensor", lambda e, ch=ch: e.matmul(
                    psT[0:1, 0:512], o128x1, sq2, start=(ch == 0), stop=(ch == 1)),
                    tok=True)
                s.wait("vector", tms)
            s.wait("vector", tms)
            tvv = s.emit("vector", lambda e: e.tensor_copy(rms_s, psT[0:1, 0:512]),
                         tok=True)
            s.wait("gpsimd", tvv)
            s.emit("gpsimd", lambda e: e.tensor_scalar(rms_s, rms_s,
                                                       1.0 / KVR, None, ALU.mult))
            tg = s.emit("gpsimd", lambda e: e.tensor_scalar(rms_s, rms_s,
                                                            eps_rms[0:1, :], None,
                                                            ALU.add), tok=True)
            tn = rsqrt_chain(rms_s, rms_r, rms_t0, rms_t1, tg)
            s.wait("tensor", tn)
            tb = s.emit("tensor", lambda e: e.matmul(psB, o1x128, rms_r,
                                                     start=True, stop=True), tok=True)
            s.wait("vector", tb)
            tvl = None
            for ch in range(2):
                tvl = s.emit("vector", lambda e, ch=ch, tsl=tsl: e.scalar_tensor_tensor(
                    kvc[:, ch * T:][:, tsl], kvc[:, ch * T:][:, tsl],
                    gkv_sb[:, ch:ch + 1], psB, ALU.mult, ALU.mult), tok=True)
            s.wait("tensor", tvl)
        return rope_toks

    A.push()
    Ke = A.alloc(T)[0:96, :]
    Ko = A.alloc(T)[0:96, :]
    Qe = A.alloc(T)[0:96, :]
    Qo = A.alloc(T)[0:96, :]
    kdst = A.alloc(512)
    Ve = A.alloc(16 * 65)
    Vo = A.alloc(16 * 65)
    PT = A.alloc(2 * 1024)
    wkd_sb = A.alloc(2 * 128)
    wvd_sb = A.alloc(2 * 128)
    wqd_sb = A.alloc(4 * 128)
    oa_sb = A.alloc(T)[0:64, :]
    ziQ = A.alloc(4)
    avst = A.alloc(64)
    vstage = A.alloc(512)
    head_state = {"oadma": None}

    def attn_head(b, h, Kb, Qb, Vb):
        pv_grp_tok = {0: None, 1: None}
        first_oa = [True]
        for sq4 in range(4):
            nk = 4 * sq4 + 4
            exp_toks = {}

            def emit_pv(k0, nk=nk):
                grp = (k0 // 2) % 2
                s.wait("tensor", exp_toks[k0])
                for j in range(2):
                    kj = k0 + j
                    for q4 in range(4):
                        s.emit("tensor", lambda e, kj=kj, q4=q4, grp=grp, nk=nk: e.matmul(
                            psPV[:, q4 * 65:(q4 + 1) * 65],
                            PT[:, grp * 1024 + (kj % 2) * 512 + q4 * 128:
                               grp * 1024 + (kj % 2) * 512 + q4 * 128 + 128],
                            Vb[:, kj * 65: kj * 65 + 65],
                            start=(kj == 0 and q4 == 0),
                            stop=(kj == nk - 1 and q4 == 3),
                            skip_group_check=True),
                            tok=(j == 1 and q4 == 3))
                pv_grp_tok[grp] = s.tok("tensor")

            for kk in range(nk):
                slot = kk % 4
                if kk >= 2 and kk % 2 == 0:
                    emit_pv(kk - 2)
                tmm = s.emit("tensor", lambda e, kk=kk, sq4=sq4, slot=slot, Kb=Kb, Qb=Qb:
                             e.matmul(
                    psS[:, slot * 512:(slot + 1) * 512], Kb[:, kk * 128:(kk + 1) * 128],
                    Qb[:, sq4 * 512:(sq4 + 1) * 512], start=True, stop=True), tok=True)
                d = 128 * kk - 512 * sq4
                if d >= 0:
                    s.wait("vector", tmm)
                    tmm = s.emit("vector", lambda e, slot=slot, d=d: e.tensor_tensor(
                        psS[:, slot * 512:(slot + 1) * 512],
                        psS[:, slot * 512:(slot + 1) * 512],
                        mstrip[:, 384 - d: 896 - d], ALU.add), tok=True)
                if kk % 2 == 1:
                    grp2 = (kk // 2) % 2
                    s.wait("scalar", tmm)
                    s.waits("scalar", [pv_grp_tok[grp2]])
                    exp_toks[kk - 1] = s.emit("scalar", lambda e, kk=kk, grp2=grp2:
                                              e.activation(
                        out=PT[:, grp2 * 1024:(grp2 + 1) * 1024],
                        in_=psS[:, ((kk - 1) % 4) * 512: ((kk - 1) % 4) * 512 + 1024],
                        func=AF.Exp, scale=SC_ATT), tok=True)
            emit_pv(nk - 2)
            tpv = pv_grp_tok[((nk - 2) // 2) % 2]
            s.wait("vector", tpv)
            if first_oa[0]:
                s.waits("vector", [head_state["oadma"]])
                first_oa[0] = False
            for q4 in range(4):
                s.emit("vector", lambda e, q4=q4: e.reciprocal(
                    ziQ[:, q4:q4 + 1], psPV[:, q4 * 65: q4 * 65 + 1]))
            gapv()
            for q4 in range(4):
                tv = s.emit("vector", lambda e, q4=q4: e.tensor_scalar(
                    avst[:, 0:64], psPV[:, q4 * 65 + 1: q4 * 65 + 65],
                    ziQ[:, q4:q4 + 1], None, ALU.mult), tok=True)
                s.wait("tensor", tv)
                ttk = transpose(psT[0:64, 0:128], avst[:, 0:64], tok=True)
                s.wait("vector", ttk)
                tv = s.emit("vector", lambda e, sq4=sq4, q4=q4: e.tensor_copy(
                    oa_sb[0:64, sq4 * 512 + q4 * 128: sq4 * 512 + q4 * 128 + 128],
                    psT[0:64, 0:128]), tok=True)
                s.wait("tensor", tv)
        tvf = s.tok("vector")
        s.wait("gpsimd", tvf)
        head_state["oadma"] = s.dma("gpsimd", oa_scr[h * 64:(h + 1) * 64, :],
                                    oa_sb[0:64, :], tok=True)

    def head_pair(b, p):
        s.wait("sync", s.tok("tensor"))
        twk = s.dma("sync", _sview(wkd_sb, 2, 128), _wview(io["wkdec"][:, p * 128:(p + 1) * 128], 2, 128))
        twv = s.dma("sync", _sview(wvd_sb, 2, 128), _wview(io["wvdec"][:, p * 128:(p + 1) * 128], 2, 128))
        twq = s.dma("sync", _sview(wqd_sb, 4, 128), _wview(io["wqdec"][:, p * 128:(p + 1) * 128], 4, 128))
        s.waits("tensor", [twk, twv, twq])
        shift_toks = []
        prev_kdma = None
        for tt4 in range(4):
            tsl = slice(tt4 * 512, (tt4 + 1) * 512)
            for ch in range(2):
                s.emit("tensor", lambda e, ch=ch, tsl=tsl: e.matmul(
                    psA, wkd_sb[:, ch * 128:(ch + 1) * 128], kvc[:, ch * T:][:, tsl],
                    start=(ch == 0), stop=(ch == 1)), tok=(ch == 1))
            tmm = s.tok("tensor")
            s.wait("vector", tmm)
            s.emit("vector", lambda e, tsl=tsl: e.tensor_copy(Ke[0:64, tsl], psA[0:64, :]))
            s.waits("vector", [prev_kdma])
            tv = s.emit("vector", lambda e: e.tensor_copy(kdst, psA), tok=True)
            s.wait("gpsimd", tv)
            prev_kdma = s.dma("gpsimd", Ko[0:64, tsl], kdst[64:128, :], tok=True)
            shift_toks.append(prev_kdma)
            s.wait("tensor", tv)
            for ch in range(4):
                s.emit("tensor", lambda e, ch=ch, tsl=tsl: e.matmul(
                    psA, wqd_sb[:, ch * 128:(ch + 1) * 128], qc[:, ch * T:][:, tsl],
                    start=(ch == 0), stop=(ch == 3)), tok=(ch == 3))
            tmm = s.tok("tensor")
            s.wait("vector", tmm)
            s.emit("vector", lambda e, tsl=tsl: e.tensor_copy(Qe[0:64, tsl], psA[0:64, :]))
            s.waits("vector", [prev_kdma])
            tv = s.emit("vector", lambda e: e.tensor_copy(kdst, psA), tok=True)
            s.wait("gpsimd", tv)
            prev_kdma = s.dma("gpsimd", Qo[0:64, tsl], kdst[64:128, :], tok=True)
            shift_toks.append(prev_kdma)
            s.wait("tensor", tv)
            for ch in range(2):
                s.emit("tensor", lambda e, ch=ch, tsl=tsl: e.matmul(
                    psA, wvd_sb[:, ch * 128:(ch + 1) * 128], kvc[:, ch * T:][:, tsl],
                    start=(ch == 0), stop=(ch == 1)), tok=(ch == 1))
            tmm = s.tok("tensor")
            s.wait("vector", tmm)
            tv = s.emit("vector", lambda e: e.tensor_copy(vstage, psA), tok=True)
            s.wait("tensor", tv)
            for k4 in range(4):
                kk = tt4 * 4 + k4
                ttk = transpose(psT[0:128, 0:128], vstage[:, k4 * 128:(k4 + 1) * 128],
                                tok=True)
                s.wait("vector", ttk)
                s.emit("vector", lambda e, kk=kk: e.tensor_copy(
                    Ve[:, kk * 65 + 1: kk * 65 + 65], psT[:, 0:64]))
                tv = s.emit("vector", lambda e, kk=kk: e.tensor_copy(
                    Vo[:, kk * 65 + 1: kk * 65 + 65], psT[:, 64:128]), tok=True)
                s.wait("tensor", tv)
        s.wait("gpsimd", s.tok("tensor"))
        for h, Kb2, Qb2 in ((2 * p, Ke, Qe), (2 * p + 1, Ko, Qo)):
            r0 = (h // 4) * 128 + (h % 4) * 32
            shift_toks.append(s.dma("gpsimd", Kb2[64:96, :], kro_scr[r0:r0 + 32, :],
                                    tok=True))
            shift_toks.append(s.dma("gpsimd", Qb2[64:96, :], qro_scr[r0:r0 + 32, :],
                                    tok=True))
        s.emit("vector", lambda e: e.memset(
            _r(Ve, "p (k w) -> p k w", k=16, w=65)[:, :, 0:1], 1.0))
        tvm = s.emit("vector", lambda e: e.memset(
            _r(Vo, "p (k w) -> p k w", k=16, w=65)[:, :, 0:1], 1.0), tok=True)
        s.waits("tensor", shift_toks + [tvm])
        if p == 0 and b == 1 and "Ke" in taps:
            s.waits("gpsimd", shift_toks + [tvm])
            s.dma("gpsimd", tap_aps["Ke"], Ke, tok=True)
            s.dma("gpsimd", tap_aps["Qe"], Qe, tok=True)
            s.dma("gpsimd", tap_aps["Ve"], Ve, tok=True)
        attn_head(b, 2 * p, Ke, Qe, Ve)
        attn_head(b, 2 * p + 1, Ko, Qo, Vo)

    wo_sb = [A.alloc(512), A.alloc(512)]
    xres = [A.alloc(C), A.alloc(C)]
    oaT = [A.alloc(128), A.alloc(128)]
    rst = A.alloc(512)

    def wo_phase(b):
        tk_oa = s.dtok("gpsimd")
        s.wait("sync", tk_oa)
        s.wait("sync", s.tok("tensor"))
        prev_rdma = None
        wg = {}
        for tc in range(16):
            bufx = tc % 2
            s.waits("sync", [wg.get(("xres", bufx))])
            tk_x = s.dma("sync", xres[bufx],
                         io["xa"][b, tc * 128:(tc + 1) * 128, :])
            for chalf in range(2):
                pb = psA if chalf == 0 else psB
                tmm = None
                for ch in range(8):
                    bufw = ch % 2
                    s.waits("sync", [wg.get(("wo", bufw))])
                    tk_w = s.dma("sync", wo_sb[bufw],
                                 io["wo"][ch * 128:(ch + 1) * 128,
                                          chalf * 512:(chalf + 1) * 512])
                    tk_o = s.dma("sync", oaT[bufw],
                                 oa_scr[ch * 128:(ch + 1) * 128,
                                        tc * 128:(tc + 1) * 128])
                    s.waits("tensor", [tk_w, tk_o])
                    tpe = s.emit("tensor", lambda e, pb=pb, bufw=bufw, ch=ch: e.matmul(
                        pb, oaT[bufw], wo_sb[bufw], start=(ch == 0), stop=(ch == 7)),
                        tok=True)
                    wg[("wo", bufw)] = tpe
                tmm = s.tok("tensor")
                s.waits("vector", [tmm, tk_x, prev_rdma])
                tv = s.emit("vector", lambda e, pb=pb, bufx=bufx, chalf=chalf:
                            e.tensor_tensor(rst, pb,
                                            xres[bufx][:, chalf * 512:(chalf + 1) * 512],
                                            ALU.add), tok=True)
                s.wait("tensor", tv)
                s.wait("gpsimd", tv)
                if chalf == 1:
                    wg[("xres", bufx)] = tv
                prev_rdma = s.dma(
                    "gpsimd",
                    cc_ra_in.ap()[b * T + tc * 128: b * T + (tc + 1) * 128,
                                  chalf * 512:(chalf + 1) * 512],
                    rst, tok=True)

    tap("Ke", (96, T)); tap("Qe", (96, T)); tap("Ve", (128, 16 * 65))
    tk_ccra = None
    if upto in ("p2", "all"):
        for b in range(BPC):
            rt = a_phase(b)
            s.waits("tensor", rt)
            for p in range(8):
                head_pair(b, p)
            wo_phase(b)
        s.wait("gpsimd", s.dtok("gpsimd"))
        tk_ccra = s.collective("AllGather", ALU.bypass,
                               [cc_ra_in.ap().opt()], [cc_ra_out.ap().opt()])
        if tap("kvc", (128, 2 * T)) is not None:
            tv = s.tok("vector")
            s.wait("gpsimd", tv)
            s.dma("gpsimd", tap_aps["kvc"], kvc, tok=True)
        if tap("oascr", (C, T)) is not None:
            s.dma("gpsimd", tap_aps["oascr"], oa_scr, tok=True)
        if tap("rain", (BPC * T, C)) is not None:
            s.wait("gpsimd", tk_ccra)
            s.dma("gpsimd", tap_aps["rain"], cc_ra_in.ap(), tok=True)
    A.pop()
    A.pop()

    A.push()
    attF = A.alloc(2048)[0:16, :]
    attL = A.alloc(256)[0:16, :]
    am = A.alloc(1)[0:16, :]
    z16 = A.alloc(1)[0:16, :]
    zi16 = A.alloc(1)[0:16, :]
    pmy = A.alloc(256)[0:16, :]
    pfl = A.alloc(FH * JPC)[0:1, :]
    pbc = A.alloc(FH * JPC)[0:64, :]
    av = A.alloc(FH * JPC)[0:64, :]
    fwo_sb = [A.alloc(C)[0:64, :], A.alloc(C)[0:64, :]]
    g3 = {}
    aost = A.alloc(1024)
    ssc_sb = [A.alloc(128), A.alloc(128)]
    aot_sb = [A.alloc(C), A.alloc(C)]
    hti = [A.alloc(C), A.alloc(C)]
    ysb = A.alloc(C)
    ynb = A.alloc(C)
    rat = [A.alloc(C), A.alloc(C)]
    st3 = {k: A.alloc(1) for k in ("m", "ss", "v", "r", "t0", "t1")}
    st3["sq"] = A.alloc(C)

    if upto == "all":
        s.wait("tensor", s.tok("vector"))
        s.wait("gpsimd", tk_ccatt)
        tk_af = s.dma("gpsimd", _r(attF, "p (r j) -> p r j", r=NCORE, j=JPC),
                      _r(cc_att_out.ap(), "r (g j) -> g r j", g=FH, j=JPC), tok=True)
        tk_al = s.dma("gpsimd", attL,
                      _r(cc_att_in.ap()[0].unsqueeze(0), "o (g j) -> (o g) j",
                         g=FH, j=JPC), tok=True)
        s.waits("vector", [tk_af])
        tv = s.emit("vector", lambda e: e.reduce_max(am, attF, axis=AX.X, negate=True),
                    tok=True)
        s.wait("gpsimd", tv)
        tg = s.emit("gpsimd", lambda e: e.tensor_scalar(am, am, SC_FFN, None, ALU.mult),
                    tok=True)
        s.waits("scalar", [tg, tk_al])
        ta = s.emit("scalar", lambda e: e.activation(out=attF, in_=attF, func=AF.Exp,
                                                     scale=SC_FFN, bias=am,
                                                     accum_out=z16), tok=True)
        ta2 = s.emit("scalar", lambda e: e.activation(out=pmy, in_=attL, func=AF.Exp,
                                                      scale=SC_FFN, bias=am), tok=True)
        s.waits("vector", [ta, ta2])
        s.emit("vector", lambda e: e.reciprocal(zi16, z16))
        gapv()
        tv = s.emit("vector", lambda e: e.tensor_scalar(pmy, pmy, zi16, None, ALU.mult),
                    tok=True)
        s.wait("gpsimd", tv)
        tk_p1 = s.dma("gpsimd", p_scr, pmy, tok=True)
        s.wait("gpsimd", tk_p1)
        tk_p2 = s.dma("gpsimd", pfl, _r(p_scr, "g j -> (g j)").unsqueeze(0), tok=True)
        s.wait("tensor", tk_p2)
        for i8 in range(8):
            tmm = s.emit("tensor", lambda e, i8=i8: e.matmul(
                psB[0:64, :], o1x64, pfl[:, i8 * 512:(i8 + 1) * 512],
                start=True, stop=True), tok=True)
            s.wait("vector", tmm)
            tv = s.emit("vector", lambda e, i8=i8: e.tensor_copy(
                pbc[:, i8 * 512:(i8 + 1) * 512], psB[0:64, :]), tok=True)
            s.wait("tensor", tv)
        tvav = s.emit("vector", lambda e: e.tensor_tensor(av, vall[0:64, :], pbc,
                                                          ALU.mult), tok=True)
        prev_adma = None
        for g in range(FH):
            bufw = g % 2
            s.waits("sync", [g3.get(("fwo", bufw))])
            tk_w = s.dma("sync", fwo_sb[bufw], io["fwo"][g])
            s.waits("tensor", [tk_w, tvav])
            for jh in range(2):
                tmm = None
                for chalf in range(2):
                    pb = psA if chalf == 0 else psB
                    s.emit("tensor", lambda e, g=g, jh=jh, chalf=chalf, pb=pb, bufw=bufw:
                           e.matmul(pb,
                                    av[:, g * 256 + jh * 128: g * 256 + jh * 128 + 128],
                                    fwo_sb[bufw][:, chalf * 512:(chalf + 1) * 512],
                                    start=True, stop=True), tok=(chalf == 1))
                tmm = s.tok("tensor")
                g3[("fwo", bufw)] = tmm
                s.waits("vector", [tmm, prev_adma])
                tv2 = None
                for chalf in range(2):
                    pb = psA if chalf == 0 else psB
                    tv2 = s.emit("vector", lambda e, pb=pb, chalf=chalf: e.tensor_tensor(
                        aost[:, chalf * 512:(chalf + 1) * 512], pb,
                        bob[:, chalf * 512:(chalf + 1) * 512], ALU.add),
                        tok=(chalf == 1))
                s.wait("tensor", tv2)
                s.wait("gpsimd", tv2)
                dstv = _r(ao_scr, "(j g) c -> j g c", j=JPC, g=FH)[
                    jh * 128:(jh + 1) * 128, g:g + 1, :]
                prev_adma = s.dma("gpsimd", dstv, aost.unsqueeze(1), tok=True)
        tk_ao = s.dtok("gpsimd")

        s.wait("tensor", s.tok("vector"))
        s.wait("sync", tk_ao)
        s.wait("gpsimd", tk_ccra)
        prev_odma = None
        prev_yadd = None
        dbg_toks = {"y": None, "f": None}
        ra_flat = cc_ra_out.ap()
        for t in range(32):
            bt = t % 2
            s.waits("sync", [g3.get(("tile", bt))])
            tk_s = s.dma("sync", ssc_sb[bt], ssc_scr[t])
            tk_a = s.dma("sync", aot_sb[bt], ao_scr[t * 128:(t + 1) * 128, :])
            tk_h = s.dma("sync", hti[bt], h_scr[t * 128:(t + 1) * 128, :])
            s.waits("gpsimd", [g3.get(("rat", bt))])
            tk_r = s.idma(rat[bt], ra_flat, ridx_sb[:, t:t + 1])
            s.waits("tensor", [tk_s, tk_a, prev_yadd])
            for chalf in range(2):
                pb = psA if chalf == 0 else psB
                s.emit("tensor", lambda e, bt=bt, chalf=chalf, pb=pb: e.matmul(
                    pb, ssc_sb[bt], aot_sb[bt][:, chalf * 512:(chalf + 1) * 512],
                    start=True, stop=True), tok=(chalf == 1))
            tmm = s.tok("tensor")
            s.waits("vector", [tmm, tk_h])
            tv = None
            s.waits("vector", [dbg_toks["y"]])
            for chalf in range(2):
                pb = psA if chalf == 0 else psB
                tv = s.emit("vector", lambda e, bt=bt, chalf=chalf, pb=pb: e.tensor_tensor(
                    ysb[:, chalf * 512:(chalf + 1) * 512], pb,
                    hti[bt][:, chalf * 512:(chalf + 1) * 512], ALU.add),
                    tok=(chalf == 1))
            s.wait("tensor", tv)
            prev_yadd = tv
            g3[("tile", bt)] = tv
            if "dbgy" in taps:
                s.wait("gpsimd", tv)
                dbg_toks["y"] = s.dma("gpsimd", dbg_y[t * 128:(t + 1) * 128, :], ysb,
                                      tok=True)
            s.waits("vector", [prev_odma, dbg_toks["f"]])
            tk_n = layernorm(ysb, ynb, st3, C, eps_ln)
            s.waits("vector", [tk_n, tk_r])
            s.emit("vector", lambda e: e.tensor_tensor(ynb, ynb, lngb, ALU.mult))
            tfa = s.emit("vector", lambda e: e.tensor_tensor(ynb, ynb, lnbb, ALU.add),
                         tok=True)
            if "dbgy" in taps:
                s.wait("gpsimd", tfa)
                dbg_toks["f"] = s.dma("gpsimd", dbg_f[t * 128:(t + 1) * 128, :], ynb,
                                      tok=True)
            tv = s.emit("vector", lambda e, bt=bt: e.tensor_tensor(ynb, ynb, rat[bt],
                                                                   ALU.add), tok=True)
            g3[("rat", bt)] = tv
            s.wait("gpsimd", tv)
            outT = _r(out, "i j c -> j i c")
            for jr in range(8):
                prev_odma = s.dma("gpsimd", outT[t * 8 + jr],
                                  ynb[jr * 16:(jr + 1) * 16, :], tok=True)
        if tap("pmy", (16, JPC)) is not None:
            tv = s.tok("vector")
            s.wait("gpsimd", tv)
            s.dma("gpsimd", tap_aps["pmy"], pmy, tok=True)
        if tap("av", (64, FH * JPC)) is not None:
            s.dma("gpsimd", tap_aps["av"], av, tok=True)
        if tap("aoscr", (FROWS, C)) is not None:
            s.dma("gpsimd", tap_aps["aoscr"], ao_scr, tok=True)
        if tap("attf", (16, 2048)) is not None:
            s.dma("gpsimd", tap_aps["attf"], attF, tok=True)
        if tap("ssc", (32 * 128, 128)) is not None:
            s.dma("gpsimd", tap_aps["ssc"], _r(ssc_scr, "t p c -> (t p) c"), tok=True)
        if tap("dbgy", (FROWS, C)) is not None:
            s.dma("gpsimd", tap_aps["dbgy"], dbg_y, tok=True)
        if tap("dbgf", (FROWS, C)) is not None:
            s.dma("gpsimd", tap_aps["dbgf"], dbg_f, tok=True)
        s.wait("gpsimd", s.dtok("gpsimd"))
    A.pop()

    with nc.Block() as block:
        @block.sync
        def _(eng):
            for f in s.ops["sync"]:
                f(eng)
        @block.tensor
        def _(eng):
            for f in s.ops["tensor"]:
                f(eng)
        @block.vector
        def _(eng):
            for f in s.ops["vector"]:
                f(eng)
        @block.scalar
        def _(eng):
            for f in s.ops["scalar"]:
                f(eng)
        @block.gpsimd
        def _(eng):
            for f in s.ops["gpsimd"]:
                f(eng)

    return nc, sorted(tap_aps)


def make_in_maps(inputs):
    consts = host_constants()
    x = np.ascontiguousarray(inputs["L0_attn_x"], dtype=np.float32)
    in_maps = []
    for r in range(NCORE):
        m = {
            "xa": np.ascontiguousarray(x[2 * r:2 * r + 2]),
            "xf": np.ascontiguousarray(
                x[:, r * JPC:(r + 1) * JPC, :].transpose(1, 0, 2).reshape(FROWS, C)),
            "wkv": inputs["L0_attn_W_kv"], "gkv": inputs["L0_attn_g_kv"],
            "wkdec": inputs["L0_attn_W_kdec"], "wvdec": inputs["L0_attn_W_vdec"],
            "wq": inputs["L0_attn_W_q"], "wqdec": inputs["L0_attn_W_qdec"],
            "wkrope": inputs["L0_attn_W_krope"], "wqrope": inputs["L0_attn_W_qrope"],
            "wo": inputs["L0_attn_W_o"],
            "fwg": inputs["L0_ffn_Wg"], "fwq": inputs["L0_ffn_Wq"],
            "fbqT": np.ascontiguousarray(np.asarray(inputs["L0_ffn_bq"]).T),
            "fwkv": inputs["L0_ffn_Wkv"],
            "fbkvT": np.ascontiguousarray(np.asarray(inputs["L0_ffn_bkv"]).T),
            "fwo": inputs["L0_ffn_Wo"], "fbo": inputs["L0_ffn_bo"],
            "flng": inputs["L0_ffn_ln_g"], "flnb": inputs["L0_ffn_ln_b"],
            "ridx": host_ridx(r),
        }
        m.update(consts)
        in_maps.append({k: np.ascontiguousarray(np.asarray(v, dtype=np.uint32)
                                                if k == "ridx"
                                                else np.asarray(v, dtype=np.float32))
                        for k, v in m.items()})
    return in_maps


def kernel(**inputs):
    nc, _ = build_program("all")
    in_maps = make_in_maps(inputs)
    res = run_bass_kernel_spmd(nc, in_maps, core_ids=list(range(NCORE)))
    shards = [res.results[r]["out"] for r in range(NCORE)]
    return np.ascontiguousarray(np.concatenate(shards, axis=1))


# version 21
# speedup vs baseline: 1.0602x; 1.0602x over previous
import os
os.environ.setdefault("TRNINF_ENABLE_CUSTOMCOMMS_RDH_AG", "1")
os.environ.setdefault("TRNINF_ENABLE_CUSTOMCOMMS_RDH_AR", "1")

import math
import numpy as np

import concourse.bass as bass
import concourse.mybir as mybir
from concourse.bass_utils import run_bass_kernel_spmd

F32 = mybir.dt.float32
U32 = mybir.dt.uint32
ALU = mybir.AluOpType
AF = mybir.ActivationFunctionType
AX = mybir.AxisListType

NCORE = 8
B, T, C = 16, 2048, 1024
H, HD, RD = 16, 64, 32
KVR, QR = 256, 512
BPC = B // NCORE
JPC = T // NCORE
FROWS = B * JPC
FH, FDH = 16, 64
SC_ATT = 1.0 / math.sqrt(HD + RD)
SC_FFN = 1.0 / math.sqrt(FDH)

ENGINES = ("sync", "tensor", "vector", "scalar", "gpsimd")


class Sched:
    def __init__(self, nc):
        self.nc = nc
        self.ops = {e: [] for e in ENGINES}
        self.sem = {e: nc.alloc_semaphore(f"sem_{e}") for e in ENGINES}
        self.cnt = {e: 0 for e in ENGINES}
        self.dsem = {e: nc.alloc_semaphore(f"dsem_{e}") for e in ("sync", "gpsimd", "scalar")}
        self.dcnt = {e: 0 for e in ("sync", "gpsimd", "scalar")}
        self.ccsem = nc.alloc_semaphore("sem_cc")
        self.cccnt = 0

    def emit(self, eng, fn, tok=False):
        if tok:
            self.cnt[eng] += 1
            sem, val = self.sem[eng], self.cnt[eng]
            def wrapped(e, fn=fn, sem=sem):
                fn(e).then_inc(sem, 1)
            self.ops[eng].append(wrapped)
            return (eng, "c", val)
        self.ops[eng].append(fn)
        return None

    def tok(self, eng):
        return (eng, "c", self.cnt[eng])

    def dtok(self, eng):
        return (eng, "d", self.dcnt[eng])

    def dma(self, eng, out, in_, tok=True, **kw):
        if tok:
            self.dcnt[eng] += 16
            sem, val = self.dsem[eng], self.dcnt[eng]
            def wrapped(e, out=out, in_=in_, sem=sem, kw=kw):
                e.dma_start(out=out, in_=in_, **kw).then_inc(sem, 16)
            self.ops[eng].append(wrapped)
            return (eng, "d", val)
        def plain(e, out=out, in_=in_, kw=kw):
            e.dma_start(out=out, in_=in_, **kw)
        self.ops[eng].append(plain)
        return None

    def idma(self, out, in_, idx_ap):
        self.dcnt["gpsimd"] += 16
        sem, val = self.dsem["gpsimd"], self.dcnt["gpsimd"]
        def wrapped(e, out=out, in_=in_, idx_ap=idx_ap, sem=sem):
            e.indirect_dma_start(
                out=out, out_offset=None, in_=in_,
                in_offset=bass.IndirectOffsetOnAxis(ap=idx_ap, axis=0),
            ).then_inc(sem, 16)
        self.ops["gpsimd"].append(wrapped)
        return ("gpsimd", "d", val)

    def collective(self, kind, op, ins, outs):
        self.cccnt += 1
        val = self.cccnt
        sem = self.ccsem
        def fn(e, kind=kind, op=op, ins=ins, outs=outs, sem=sem):
            e.collective_compute(
                kind, op, replica_groups=[list(range(NCORE))],
                ins=ins, outs=outs,
            ).then_inc(sem, 1)
        self.ops["gpsimd"].append(fn)
        return ("cc", "c", val)

    def wait(self, eng, tk):
        if tk is None:
            return
        src, kind, val = tk
        sem = self.ccsem if src == "cc" else (self.dsem[src] if kind == "d" else self.sem[src])
        self.ops[eng].append(lambda e, sem=sem, val=val: e.wait_ge(sem, val))

    def waits(self, eng, toks):
        best = {}
        for t in toks:
            if t is None:
                continue
            k = (t[0], t[1])
            if k not in best or best[k][2] < t[2]:
                best[k] = t
        for t in best.values():
            self.wait(eng, t)


class Arena:
    def __init__(self, nc, words):
        self.t = nc.alloc_sbuf_tensor("arena", [128, words], F32).ap()
        self.words = words
        self.off = 0
        self.marks = []

    def alloc(self, words, align=128):
        off = -(-self.off // align) * align
        assert off + words <= self.words, f"arena overflow {off}+{words}>{self.words}"
        self.off = off + words
        return self.t[:, off:off + words]

    def push(self):
        self.marks.append(self.off)

    def pop(self):
        self.off = self.marks.pop()


def host_constants():
    c = {}
    c["ident"] = np.eye(128, dtype=np.float32)
    g = np.arange(896)[None, :]
    p = np.arange(128)[:, None]
    c["mstrip"] = np.where(g < p + 384, -1e30, 0.0).astype(np.float32)
    inv_freq = 1.0 / 10000.0 ** (np.arange(0, RD, 2, dtype=np.float32) / RD)
    t = np.arange(T, dtype=np.float32)
    freqs = np.outer(t, inv_freq)
    emb = np.concatenate([freqs, freqs], -1)
    c["coss"] = np.tile(np.cos(emb).T, (4, 1)).astype(np.float32)
    c["sins"] = np.tile(np.sin(emb).T, (4, 1)).astype(np.float32)
    P = np.zeros((128, 128), dtype=np.float32)
    for blk in range(4):
        b0 = blk * 32
        for d in range(16):
            P[b0 + d + 16, b0 + d] = -1.0
            P[b0 + d, b0 + d + 16] = 1.0
    c["pmat"] = P
    repj = np.zeros((8, 128), dtype=np.float32)
    for j in range(8):
        repj[j, j * 16:(j + 1) * 16] = 1.0
    c["repj"] = repj
    reg = np.zeros((16, 128), dtype=np.float32)
    for j in range(8):
        for g in range(16):
            reg[g, j * 16 + g] = 1.0
    c["reg"] = reg
    mj = np.zeros((128, 128), dtype=np.float32)
    for j in range(8):
        mj[j * 16:(j + 1) * 16, j * 16:(j + 1) * 16] = 1.0
    c["mj"] = mj
    c["ones_1x64"] = np.ones((1, 64), dtype=np.float32)
    c["ones_1x128"] = np.ones((1, 128), dtype=np.float32)
    c["ones_128x1"] = np.ones((128, 1), dtype=np.float32)
    return c


def host_ridx(rank):
    idx = np.zeros((128, 32), dtype=np.uint32)
    for t in range(32):
        for jr in range(8):
            for i in range(16):
                idx[jr * 16 + i, t] = i * T + rank * JPC + 8 * t + jr
    return idx


def _r(ap, spec, **kw):
    return ap.rearrange(spec, **kw)


def _wview(hbm_ap, a, b):
    return _r(hbm_ap, "(a b) c -> b a c", a=a, b=b)


def _sview(sb_ap, a, c):
    return _r(sb_ap, "p (a c) -> p a c", a=a, c=c)


def build_program(upto="all", taps=()):
    taps = set(taps)
    nc = bass.Bass("TRN2", target_bir_lowering=False)

    def din(name, shape, dt=F32):
        return nc.dram_tensor(name, list(shape), dt, kind="ExternalInput").ap()

    io = {}
    io["xa"] = din("xa", (BPC, T, C))
    io["xf"] = din("xf", (FROWS, C))
    for n, sh in [("wkv", (C, KVR)), ("gkv", (KVR,)), ("wkdec", (KVR, C)),
                  ("wvdec", (KVR, C)), ("wq", (C, QR)), ("wqdec", (QR, C)),
                  ("wkrope", (C, H * RD)), ("wqrope", (QR, H * RD)), ("wo", (C, C)),
                  ("fwg", (C, FH)), ("fwq", (FH, C, FDH)), ("fbqT", (FDH, FH)),
                  ("fwkv", (FH, C, 2 * FDH)), ("fbkvT", (2 * FDH, FH)),
                  ("fwo", (FH, FDH, C)), ("fbo", (C,)), ("flng", (C,)), ("flnb", (C,)),
                  ("ident", (128, 128)), ("mstrip", (128, 896)), ("coss", (128, T)),
                  ("sins", (128, T)), ("pmat", (128, 128)), ("ones_1x64", (1, 64)),
                  ("ones_1x128", (1, 128)), ("ones_128x1", (128, 1)),
                  ("repj", (8, 128)), ("reg", (16, 128)), ("mj", (128, 128))]:
        io[n] = din(n, sh)
    io["ridx"] = din("ridx", (128, 32), U32)

    out = nc.dram_tensor("out", [B, JPC, C], F32, kind="ExternalOutput").ap()

    h_scr = nc.dram_tensor("h_scr", [FROWS, C], F32).ap()
    ssc_scr = nc.dram_tensor("ssc_scr", [32, 128, 128], F32).ap()
    kro_scr = nc.dram_tensor("kro_scr", [H * RD, T], F32).ap()
    qro_scr = nc.dram_tensor("qro_scr", [H * RD, T], F32).ap()
    oa_scr = nc.dram_tensor("oa_scr", [C, T], F32).ap()
    ao_scr = nc.dram_tensor("ao_scr", [FROWS, C], F32).ap()
    p_scr = nc.dram_tensor("p_scr", [FH, JPC], F32).ap()
    dbg_y = nc.dram_tensor("dbg_y", [FROWS, C], F32).ap()
    dbg_f = nc.dram_tensor("dbg_f", [FROWS, C], F32).ap()
    cc_att_in = nc.dram_tensor("cc_att_in", [1, FH * JPC], F32)
    cc_att_out = nc.dram_tensor("cc_att_out", [NCORE, FH * JPC], F32, addr_space="Shared")
    cc_ra_in = nc.dram_tensor("cc_ra_in", [BPC * T, C], F32)
    cc_ra_out = nc.dram_tensor("cc_ra_out", [NCORE * BPC * T, C], F32, addr_space="Shared")

    tap_aps = {}
    def tap(name, shape):
        if name in taps:
            tap_aps[name] = nc.dram_tensor("tap_" + name, list(shape), F32,
                                           kind="ExternalOutput").ap()
        return tap_aps.get(name)

    s = Sched(nc)
    A = Arena(nc, 50944)

    ident = A.alloc(128)
    mstrip = A.alloc(896)
    pmat = A.alloc(128)
    coss = A.alloc(T)
    sins = A.alloc(T)
    o1x64 = A.alloc(64)[0:1, :]
    o1x128 = A.alloc(128)[0:1, :]
    o128x1 = A.alloc(1)
    gkv_sb = A.alloc(2)
    fbqT_sb = A.alloc(FH)
    fbkvT_sb = A.alloc(FH)
    lngb = A.alloc(C)
    lnbb = A.alloc(C)
    bob = A.alloc(C)
    eps_ln = A.alloc(1)
    eps_rms = A.alloc(1)
    gap1 = A.alloc(8)
    repj_sb = A.alloc(128)[0:8, :]
    reg_sb = A.alloc(128)[0:16, :]
    mj_sb = A.alloc(128)
    ridx_sb = A.alloc(32).bitcast(U32)
    attL_p = A.alloc(JPC)[0:16, :]
    vall = A.alloc(FH * JPC)

    psS = nc.alloc_psum_tensor("psS", [128, 2048], F32).ap()
    psPV = nc.alloc_psum_tensor("psPV", [128, 512], F32).ap()
    psA = nc.alloc_psum_tensor("psA", [128, 512], F32).ap()
    psB = nc.alloc_psum_tensor("psB", [128, 512], F32).ap()
    psT = nc.alloc_psum_tensor("psT", [128, 512], F32).ap()

    ctoks = [
        s.dma("sync", ident, io["ident"]),
        s.dma("sync", mstrip, io["mstrip"]),
        s.dma("sync", pmat, io["pmat"]),
        s.dma("sync", coss, io["coss"]),
        s.dma("sync", sins, io["sins"]),
        s.dma("sync", o1x64, io["ones_1x64"]),
        s.dma("sync", o1x128, io["ones_1x128"]),
        s.dma("sync", o128x1, io["ones_128x1"]),
        s.dma("sync", gkv_sb, _r(io["gkv"], "(a b) -> b a", a=2, b=128),
              allow_slow_non_contiguous=True),
        s.dma("sync", fbqT_sb[0:FDH, :], io["fbqT"]),
        s.dma("sync", fbkvT_sb, io["fbkvT"]),
        s.dma("sync", ridx_sb, io["ridx"]),
        s.dma("sync", repj_sb, io["repj"]),
        s.dma("sync", reg_sb, io["reg"]),
        s.dma("sync", mj_sb, io["mj"]),
    ]

    A.push()
    row3 = A.alloc(3 * C)
    r_flng, r_flnb, r_fbo = row3[0:1, 0:C], row3[0:1, C:2 * C], row3[0:1, 2 * C:3 * C]
    ctoks.append(s.dma("sync", r_flng, io["flng"].unsqueeze(0)))
    ctoks.append(s.dma("sync", r_flnb, io["flnb"].unsqueeze(0)))
    ctoks.append(s.dma("sync", r_fbo, io["fbo"].unsqueeze(0)))
    s.waits("tensor", ctoks)
    for row, dst in ((r_flng, lngb), (r_flnb, lnbb), (r_fbo, bob)):
        for hf in range(2):
            tmm = s.emit("tensor", lambda e, row=row, hf=hf: e.matmul(
                psB, o1x128, row[:, hf * 512:(hf + 1) * 512], start=True, stop=True),
                tok=True)
            s.wait("vector", tmm)
            tv = s.emit("vector", lambda e, dst=dst, hf=hf: e.tensor_copy(
                dst[:, hf * 512:(hf + 1) * 512], psB), tok=True)
            s.wait("tensor", tv)
    s.emit("vector", lambda e: e.memset(eps_ln, 1e-5))
    s.emit("vector", lambda e: e.memset(eps_rms, 1e-6))
    s.emit("vector", lambda e: e.memset(gap1, 0.0))
    s.wait("sync", s.tok("tensor"))
    A.pop()

    def gapv():
        s.emit("vector", lambda e: e.tensor_copy(gap1[:, 0:4], gap1[:, 4:8]))

    def rsqrt_chain(v_ap, r_ap, t0_ap, t1_ap, pre_tok):
        s.wait("scalar", pre_tok)
        s.emit("scalar", lambda e: e.activation(out=r_ap, in_=v_ap, func=AF.Ln))
        ta = s.emit("scalar", lambda e: e.activation(out=r_ap, in_=r_ap,
                                                     func=AF.Exp, scale=-0.5), tok=True)
        s.wait("gpsimd", ta)
        s.emit("gpsimd", lambda e: e.tensor_tensor(t0_ap, r_ap, r_ap, ALU.mult))
        s.emit("gpsimd", lambda e: e.tensor_tensor(t1_ap, v_ap, t0_ap, ALU.mult))
        s.emit("gpsimd", lambda e: e.tensor_scalar(t1_ap, t1_ap, -0.5, 1.5,
                                                   ALU.mult, op1=ALU.add))
        return s.emit("gpsimd", lambda e: e.tensor_tensor(r_ap, r_ap, t1_ap,
                                                          ALU.mult), tok=True)

    def layernorm(xin, xout, st, D, eps_tile):
        s.emit("vector", lambda e: e.tensor_scalar(st["sq"], xin, 1.0, 0.0,
                                                   ALU.mult, op1=ALU.add,
                                                   accum_out=st["m"]))
        s.emit("vector", lambda e: e.tensor_tensor(st["sq"], xin, xin, ALU.mult))
        tv = s.emit("vector", lambda e: e.tensor_scalar(st["sq"], st["sq"], 1.0, 0.0,
                                                        ALU.mult, op1=ALU.add,
                                                        accum_out=st["ss"]), tok=True)
        s.wait("gpsimd", tv)
        s.emit("gpsimd", lambda e: e.tensor_scalar(st["m"], st["m"], 1.0 / D, None, ALU.mult))
        s.emit("gpsimd", lambda e: e.tensor_scalar(st["v"], st["ss"], 1.0 / D, None, ALU.mult))
        s.emit("gpsimd", lambda e: e.tensor_tensor(st["t0"], st["m"], st["m"], ALU.mult))
        s.emit("gpsimd", lambda e: e.tensor_tensor(st["v"], st["v"], st["t0"], ALU.subtract))
        tg = s.emit("gpsimd", lambda e: e.tensor_scalar(st["v"], st["v"], eps_tile,
                                                        None, ALU.add), tok=True)
        tn = rsqrt_chain(st["v"], st["r"], st["t0"], st["t1"], tg)
        s.wait("vector", tn)
        return s.emit("vector", lambda e: e.tensor_scalar(xout, xin, st["m"], st["r"],
                                                          ALU.subtract, op1=ALU.mult),
                      tok=True)

    def transpose(dst_psum, src, tok=False):
        p = src.partition_size()
        idn = ident if p == 128 else ident[0:p, 0:p]
        return s.emit("tensor", lambda e, dst_psum=dst_psum, src=src, idn=idn:
                      e.transpose(dst_psum, src, idn), tok=tok)

    A.push()
    xr = [A.alloc(C), A.alloc(C)]
    hrow = [A.alloc(C), A.alloc(C)]
    st1 = {k: A.alloc(1) for k in ("m", "ss", "v", "r", "t0", "t1")}
    st1["sq"] = A.alloc(C)
    hfm = A.alloc(8 * 128)
    lfm = A.alloc(128)[0:16, :]
    ltm = A.alloc(16)
    nmx = A.alloc(1)
    zsum = A.alloc(1)
    ziv = A.alloc(1)
    probs = A.alloc(16)
    prT = A.alloc(128)[0:16, :]
    smax = A.alloc(8)[0:16, :]
    smaxT = A.alloc(16)[0:8, :]
    scol = A.alloc(16)
    scolT = A.alloc(128)[0:16, :]
    sgf = A.alloc(32 * 128)
    sst = A.alloc(128)
    hg = A.alloc(8 * 2048)
    gst = A.alloc(1024)
    fwq_sb = [A.alloc(8 * FDH), A.alloc(8 * FDH)]
    fwkv_sb = [A.alloc(8 * 2 * FDH), A.alloc(8 * 2 * FDH)]
    fwg_sb = A.alloc(8 * FH)
    qg = A.alloc(JPC)
    kg = A.alloc(JPC)
    kvst = A.alloc(JPC)
    dotst = A.alloc(JPC)[0:1, :]

    tk_fwg = s.dma("sync", _sview(fwg_sb, 8, FH), _wview(io["fwg"], 8, 128))
    hg_view = _r(hg, "p (c g j) -> p c g j", c=8, g=8, j=JPC)
    guards = {}
    p1_bank = {}

    def p1_tile(t, pass2):
        buf = t % 2
        if not pass2:
            s.waits("sync", [guards.get(("xr", buf))])
            tk_x = s.dma("sync", xr[buf], io["xf"][t * 128:(t + 1) * 128, :])
            s.waits("vector", [tk_x, guards.get(("hrow", buf))])
            tk_h = layernorm(xr[buf], hrow[buf], st1, C, eps_ln)
            guards[("xr", buf)] = tk_h
            s.wait("gpsimd", tk_h)
            s.dma("gpsimd", h_scr[t * 128:(t + 1) * 128, :], hrow[buf], tok=True)
            s.wait("tensor", tk_h)
            for half in range(2):
                pb = psT if half == 0 else psB
                s.waits("tensor", [p1_bank.get(id(pb))])
                for c4 in range(4):
                    ch = half * 4 + c4
                    tt = transpose(pb[:, c4 * 128:(c4 + 1) * 128],
                                   hrow[buf][:, ch * 128:(ch + 1) * 128],
                                   tok=(c4 == 3))
                s.wait("vector", tt)
                tv = s.emit("vector", lambda e, half=half, pb=pb: e.tensor_copy(
                    hfm[:, half * 512:(half + 1) * 512], pb[:, 0:512]), tok=True)
                p1_bank[id(pb)] = tv
            s.waits("tensor", [tk_fwg, p1_bank.get(id(psT)), p1_bank.get(id(psB)),
                               p1_bank.get(id(psA))])
            for ch in range(8):
                s.emit("tensor", lambda e, ch=ch: e.matmul(
                    psA[0:16, 0:128], fwg_sb[:, ch * 16:(ch + 1) * 16],
                    hfm[:, ch * 128:(ch + 1) * 128], start=(ch == 0), stop=(ch == 7)),
                    tok=(ch == 7))
            tmm = s.tok("tensor")
            s.wait("vector", tmm)
            tv = s.emit("vector", lambda e: e.tensor_copy(lfm, psA[0:16, 0:128]), tok=True)
            s.wait("tensor", tv)
            tt = transpose(psT[0:128, 0:16], lfm, tok=True)
            s.wait("vector", tt)
            s.emit("vector", lambda e: e.tensor_copy(ltm, psT[:, 0:16]))
            gapv()
            s.emit("vector", lambda e: e.reduce_max(nmx, ltm, axis=AX.X, negate=True))
            gapv()
            tv2 = s.emit("vector", lambda e: e.tensor_copy(gap1[:, 0:4], gap1[:, 4:8]),
                         tok=True)
            s.wait("scalar", tv2)
            ta = s.emit("scalar", lambda e: e.activation(out=probs, in_=ltm, func=AF.Exp,
                                                         bias=nmx, accum_out=zsum),
                        tok=True)
            s.wait("vector", ta)
            s.emit("vector", lambda e: e.reciprocal(ziv, zsum))
            gapv()
            tv = s.emit("vector", lambda e: e.tensor_scalar(probs, probs, ziv, None,
                                                            ALU.mult), tok=True)
            s.wait("tensor", tv)
            tt = transpose(psT[0:16, 0:128], probs, tok=True)
            s.wait("vector", tt)
            tv = s.emit("vector", lambda e: e.tensor_copy(prT, psT[0:16, 0:128]),
                        tok=True)
            gapv()
            prv = _r(prT, "p (j i) -> p j i", j=8, i=16)
            tv2 = s.emit("vector", lambda e, prv=prv: e.reduce_max(smax, prv, axis=AX.X),
                         tok=True)
            s.wait("tensor", tv2)
            tt = transpose(psB[0:8, 0:16], smax, tok=True)
            s.wait("vector", tt)
            tv = s.emit("vector", lambda e: e.tensor_copy(smaxT, psB[0:8, 0:16]),
                        tok=True)
            s.wait("tensor", tv)
            tmm = s.emit("tensor", lambda e: e.matmul(psB[:, 0:16], repj_sb, smaxT,
                                                      start=True, stop=True), tok=True)
            s.wait("vector", tmm)
            tv = s.emit("vector", lambda e: e.tensor_tensor(scol, probs, psB[:, 0:16],
                                                            ALU.is_equal), tok=True)
            s.wait("tensor", tv)
            tt = transpose(psB[0:16, 0:128], scol, tok=True)
            s.wait("vector", tt)
            tv = s.emit("vector", lambda e: e.tensor_copy(scolT, psB[0:16, 0:128]),
                        tok=True)
            s.wait("tensor", tv)
            tmm = s.emit("tensor", lambda e: e.matmul(psB[:, 0:128], reg_sb, scolT,
                                                      start=True, stop=True), tok=True)
            s.wait("vector", tmm)
            tv = s.emit("vector", lambda e: e.tensor_tensor(sst, psB[:, 0:128], mj_sb, ALU.mult),
                        tok=True)
            s.wait("gpsimd", tv)
            s.dma("gpsimd", ssc_scr[t], sst, tok=True)
            s.wait("tensor", tv)
            tt = transpose(psB[:, 0:128], sst, tok=True)
            s.wait("vector", tt)
            tsg = s.emit("vector", lambda e, t=t: e.tensor_copy(
                sgf[:, t * 128:(t + 1) * 128], psB[:, 0:128]), tok=True)
            s.wait("tensor", tsg)
            hf0 = 0
        else:
            s.waits("sync", [guards.get(("hrow", buf))])
            tk_x = s.dma("sync", hrow[buf], h_scr[t * 128:(t + 1) * 128, :])
            s.wait("tensor", tk_x)
            hf0 = 1
        s.waits("tensor", [p1_bank.get(id(psA)), p1_bank.get(id(psB))])
        for cf in range(2):
            s.emit("tensor", lambda e, t=t, cf=cf, buf=buf: e.matmul(
                psA if cf == 0 else psB,
                sgf[:, t * 128:(t + 1) * 128],
                hrow[buf][:, cf * 512:(cf + 1) * 512], start=True, stop=True),
                tok=(cf == 1))
        tmm = s.tok("tensor")
        guards[("hrow", buf)] = tmm
        s.wait("vector", tmm)
        s.waits("vector", [p1_bank.get(id(psB))])
        s.emit("vector", lambda e: e.tensor_copy(gst[:, 0:512], psA))
        tv = s.emit("vector", lambda e: e.tensor_copy(gst[:, 512:1024], psB),
                    tok=True)
        p1_bank[id(psA)] = tv
        s.wait("tensor", tv)
        for half in range(2):
            pb = psT if half == 0 else psB
            s.waits("tensor", [p1_bank.get(id(pb))])
            for c4 in range(4):
                ch = half * 4 + c4
                tt = transpose(pb[:, c4 * 128:(c4 + 1) * 128],
                               gst[:, ch * 128:(ch + 1) * 128], tok=(c4 == 3))
            s.wait("vector", tt)
            srcv = _r(pb, "p (c j g) -> p c g j", c=4, j=8, g=16)[
                :, :, hf0 * 8:hf0 * 8 + 8, :]
            dst = hg_view[:, half * 4:(half + 1) * 4, :, 8 * t: 8 * t + 8]
            tv = s.emit("vector", lambda e, srcv=srcv, dst=dst: e.tensor_copy(dst, srcv),
                        tok=True)
            p1_bank[id(pb)] = tv

    def p1_per_g(half):
        prev_vdma = None
        prev_ddma = None
        prev_dot = None
        for gl in range(8):
            g = 8 * half + gl
            pb = gl % 2
            s.waits("sync", [guards.get(("fw", pb))])
            tkw1 = s.dma("sync", _sview(fwq_sb[pb], 8, FDH),
                         _wview(io["fwq"][g], 8, 128))
            tkw2 = s.dma("sync", _sview(fwkv_sb[pb], 8, 2 * FDH),
                         _wview(io["fwkv"][g], 8, 128))
            s.waits("tensor", [tkw1, tkw2])
            for ch in range(8):
                s.emit("tensor", lambda e, ch=ch, pb=pb, gl=gl: e.matmul(
                    psA[0:64, 0:256], fwq_sb[pb][:, ch * 64:(ch + 1) * 64],
                    hg_view[:, ch, gl, :], start=(ch == 0), stop=(ch == 7)),
                    tok=(ch == 7))
            tq = s.tok("tensor")
            for ch in range(8):
                s.emit("tensor", lambda e, ch=ch, pb=pb, gl=gl: e.matmul(
                    psB[:, 0:256], fwkv_sb[pb][:, ch * 128:(ch + 1) * 128],
                    hg_view[:, ch, gl, :], start=(ch == 0), stop=(ch == 7)),
                    tok=(ch == 7))
            tkv = s.tok("tensor")
            guards[("fw", pb)] = tkv
            s.waits("scalar", [tq, prev_dot])
            ta1 = s.emit("scalar", lambda e, g=g: e.activation(
                out=qg[0:64, :], in_=psA[0:64, 0:256], func=AF.Identity,
                bias=fbqT_sb[0:64, g:g + 1]), tok=True)
            s.waits("scalar", [tkv, prev_vdma])
            ta2 = s.emit("scalar", lambda e, g=g: e.activation(
                out=kg[0:64, :], in_=psB[0:64, 0:256], func=AF.Identity,
                bias=fbkvT_sb[0:64, g:g + 1]), tok=True)
            ta3 = s.emit("scalar", lambda e, g=g: e.activation(
                out=kvst[64:128, :], in_=psB[64:128, 0:256], func=AF.Identity,
                bias=fbkvT_sb[64:128, g:g + 1]), tok=True)
            s.wait("gpsimd", ta3)
            prev_vdma = s.dma("gpsimd", vall[0:64, g * 256:(g + 1) * 256],
                              kvst[64:128, :], tok=True)
            s.waits("vector", [ta1, ta2])
            tv = s.emit("vector", lambda e: e.tensor_tensor(kvst[0:64, :], qg[0:64, :],
                                                            kg[0:64, :], ALU.mult),
                        tok=True)
            s.wait("tensor", tv)
            tmm = s.emit("tensor", lambda e: e.matmul(
                psT[0:1, 0:256], o128x1[0:64, :], kvst[0:64, :], start=True, stop=True),
                tok=True)
            s.waits("vector", [tmm, prev_ddma])
            prev_dot = s.emit("vector", lambda e: e.tensor_copy(dotst[:, 0:256],
                                                                psT[0:1, 0:256]), tok=True)
            s.wait("gpsimd", prev_dot)
            prev_ddma = s.dma("gpsimd", cc_att_in.ap()[:, g * 256:(g + 1) * 256],
                              dotst[:, 0:256], tok=True)

    tk_ccatt = None
    if upto in ("p1", "p2", "all"):
        for t in range(32):
            p1_tile(t, pass2=False)
        p1_per_g(0)
        s.wait("sync", s.tok("tensor"))
        for t in range(32):
            p1_tile(t, pass2=True)
        p1_per_g(1)
        s.wait("gpsimd", s.dtok("gpsimd"))
        tk_ccatt = s.collective("AllGather", ALU.bypass,
                                [cc_att_in.ap().opt()], [cc_att_out.ap().opt()])
        s.dma("gpsimd", attL_p,
              _r(cc_att_in.ap()[0].unsqueeze(0), "o (g j) -> (o g) j", g=FH, j=JPC),
              tok=True)
        if tap("hg", (128, 8 * 2048)) is not None:
            tv = s.tok("vector")
            s.wait("gpsimd", tv)
            s.dma("gpsimd", tap_aps["hg"], hg, tok=True)
        if tap("vall", (128, FH * JPC)) is not None:
            s.dma("gpsimd", tap_aps["vall"], vall, tok=True)
        if tap("attin", (1, FH * JPC)) is not None:
            s.dma("gpsimd", tap_aps["attin"], cc_att_in.ap(), tok=True)
        if tap("hscr", (FROWS, C)) is not None:
            s.dma("gpsimd", tap_aps["hscr"], h_scr, tok=True)

    A.pop()

    A.push()
    st2 = {k: A.alloc(1) for k in ("m", "ss", "v", "r", "t0", "t1")}
    st2["sq"] = A.alloc(C)
    kvc = A.alloc(2 * T)
    qc = A.alloc(4 * T)
    A.push()
    sq2 = A.alloc(512)
    xr2 = [A.alloc(C), A.alloc(C)]
    xn2 = [A.alloc(C), A.alloc(C)]
    xnf = A.alloc(8 * 512)
    wkv_sb = A.alloc(8 * KVR)
    wq_sb = A.alloc(8 * QR)
    wro_sb = A.alloc(8 * QR)
    ro_raw = A.alloc(512)
    ro_out = A.alloc(512)
    rms_s = A.alloc(512)[0:1, :]
    rms_r = A.alloc(512)[0:1, :]
    rms_t0 = A.alloc(512)[0:1, :]
    rms_t1 = A.alloc(512)[0:1, :]
    A.pop()

    rope_state = {"prev": None}

    def rope_block(dst_scr, oc, g4, psrc, bank_ev):
        tv = s.emit("vector", lambda e, psrc=psrc: e.tensor_copy(ro_raw, psrc), tok=True)
        bank_ev[id(psrc)] = tv
        s.wait("tensor", tv)
        trot = s.emit("tensor", lambda e: e.matmul(psB, pmat, ro_raw,
                                                   start=True, stop=True), tok=True)
        tsl = slice(g4 * 512, (g4 + 1) * 512)
        s.waits("vector", [rope_state["prev"]])
        s.emit("vector", lambda e, tsl=tsl: e.tensor_tensor(
            ro_out, ro_raw, coss[:, tsl], ALU.mult))
        s.wait("vector", trot)
        s.emit("vector", lambda e, tsl=tsl: e.tensor_tensor(
            ro_raw, psB, sins[:, tsl], ALU.mult))
        gapv()
        tv = s.emit("vector", lambda e: e.tensor_tensor(ro_out, ro_out, ro_raw,
                                                        ALU.add), tok=True)
        s.wait("gpsimd", tv)
        tk = s.dma("gpsimd", dst_scr[oc * 128:(oc + 1) * 128, tsl], ro_out, tok=True)
        rope_state["prev"] = tk
        return tk

    g2 = {}
    bank_ev = {}

    def a_phase(b):
        s.wait("sync", s.tok("tensor"))
        s.wait("sync", s.dtok("gpsimd"))
        s.wait("sync", s.tok("vector"))
        tw1 = s.dma("sync", _sview(wkv_sb, 8, KVR), _wview(io["wkv"], 8, 128))
        tw2 = s.dma("sync", _sview(wq_sb, 8, QR), _wview(io["wq"], 8, 128))
        tw3 = s.dma("sync", _sview(wro_sb, 8, QR), _wview(io["wkrope"], 8, 128))
        rope_toks = []
        for g4 in range(4):
            for sub in range(4):
                buf = sub % 2
                s.waits("sync", [g2.get(("xr2", buf))])
                tk_x = s.dma("sync", xr2[buf],
                             io["xa"][b, g4 * 512 + sub * 128: g4 * 512 + (sub + 1) * 128, :])
                s.waits("vector", [tk_x, g2.get(("xn2", buf))])
                tk_n = layernorm(xr2[buf], xn2[buf], st2, C, eps_ln)
                g2[("xr2", buf)] = tk_n
                s.wait("tensor", tk_n)
                xnf_v = _r(xnf, "p (c t) -> p c t", c=8, t=512)
                for half in range(2):
                    pb = psT if half == 0 else psB
                    s.waits("tensor", [bank_ev.get(id(pb))])
                    for c4 in range(4):
                        ch = half * 4 + c4
                        tt = transpose(pb[:, c4 * 128:(c4 + 1) * 128],
                                       xn2[buf][:, ch * 128:(ch + 1) * 128],
                                       tok=(c4 == 3))
                    if half == 1:
                        g2[("xn2", buf)] = tt
                    s.wait("vector", tt)
                    tv = s.emit("vector", lambda e, half=half, sub=sub, pb=pb,
                                xnf_v=xnf_v: e.tensor_copy(
                        xnf_v[:, half * 4:(half + 1) * 4, sub * 128:sub * 128 + 128],
                        _r(pb, "p (c t) -> p c t", c=4, t=128)), tok=True)
                    bank_ev[id(pb)] = tv
            s.waits("tensor", [tw1, tw2, tw3, bank_ev.get(id(psT)),
                               bank_ev.get(id(psB))])

            def proj_into(dst, w_sb, noc, wstride):
                for oc in range(noc):
                    pb = psA if oc % 2 == 0 else psT
                    s.waits("tensor", [bank_ev.get(id(pb))])
                    for ch in range(8):
                        s.emit("tensor", lambda e, oc=oc, ch=ch, pb=pb, w_sb=w_sb,
                               wstride=wstride: e.matmul(
                            pb, w_sb[:, ch * wstride + oc * 128: ch * wstride + oc * 128 + 128],
                            xnf[:, ch * 512:(ch + 1) * 512],
                            start=(ch == 0), stop=(ch == 7)), tok=(ch == 7))
                    tmm = s.tok("tensor")
                    s.wait("vector", tmm)
                    tv = s.emit("vector", lambda e, dst=dst, oc=oc, pb=pb, g4=g4:
                                e.tensor_copy(
                                    dst[:, oc * T + g4 * 512: oc * T + g4 * 512 + 512], pb),
                                tok=True)
                    bank_ev[id(pb)] = tv

            proj_into(kvc, wkv_sb, 2, KVR)
            proj_into(qc, wq_sb, 4, QR)
            for oc in range(4):
                s.waits("tensor", [bank_ev.get(id(psA))])
                for ch in range(8):
                    s.emit("tensor", lambda e, oc=oc, ch=ch: e.matmul(
                        psA, wro_sb[:, ch * QR + oc * 128: ch * QR + oc * 128 + 128],
                        xnf[:, ch * 512:(ch + 1) * 512],
                        start=(ch == 0), stop=(ch == 7)), tok=(ch == 7))
                s.wait("vector", s.tok("tensor"))
                rope_toks.append(rope_block(kro_scr, oc, g4, psA, bank_ev))
        s.wait("sync", s.tok("tensor"))
        tw4 = s.dma("sync", _sview(wro_sb[:, 0:4 * QR], 4, QR),
                    _wview(io["wqrope"], 4, 128))
        s.waits("tensor", [tw4])
        for g4 in range(4):
            for oc in range(4):
                s.waits("tensor", [bank_ev.get(id(psA))])
                for ch in range(4):
                    s.emit("tensor", lambda e, oc=oc, ch=ch, g4=g4: e.matmul(
                        psA, wro_sb[:, ch * QR + oc * 128: ch * QR + oc * 128 + 128],
                        qc[:, ch * T + g4 * 512: ch * T + g4 * 512 + 512],
                        start=(ch == 0), stop=(ch == 3)), tok=(ch == 3))
                s.wait("vector", s.tok("tensor"))
                rope_toks.append(rope_block(qro_scr, oc, g4, psA, bank_ev))
        for g4 in range(4):
            tsl = slice(g4 * 512, (g4 + 1) * 512)
            for ch in range(2):
                tvq = s.emit("vector", lambda e, ch=ch, tsl=tsl: e.tensor_tensor(
                    sq2, kvc[:, ch * T:][:, tsl], kvc[:, ch * T:][:, tsl], ALU.mult),
                    tok=True)
                s.wait("tensor", tvq)
                tms = s.emit("tensor", lambda e, ch=ch: e.matmul(
                    psT[0:1, 0:512], o128x1, sq2, start=(ch == 0), stop=(ch == 1)),
                    tok=True)
                s.wait("vector", tms)
            s.wait("vector", tms)
            tvv = s.emit("vector", lambda e: e.tensor_copy(rms_s, psT[0:1, 0:512]),
                         tok=True)
            s.wait("gpsimd", tvv)
            s.emit("gpsimd", lambda e: e.tensor_scalar(rms_s, rms_s,
                                                       1.0 / KVR, None, ALU.mult))
            tg = s.emit("gpsimd", lambda e: e.tensor_scalar(rms_s, rms_s,
                                                            eps_rms[0:1, :], None,
                                                            ALU.add), tok=True)
            tn = rsqrt_chain(rms_s, rms_r, rms_t0, rms_t1, tg)
            s.wait("tensor", tn)
            tb = s.emit("tensor", lambda e: e.matmul(psB, o1x128, rms_r,
                                                     start=True, stop=True), tok=True)
            s.wait("vector", tb)
            tvl = None
            for ch in range(2):
                tvl = s.emit("vector", lambda e, ch=ch, tsl=tsl: e.scalar_tensor_tensor(
                    kvc[:, ch * T:][:, tsl], kvc[:, ch * T:][:, tsl],
                    gkv_sb[:, ch:ch + 1], psB, ALU.mult, ALU.mult), tok=True)
            s.wait("tensor", tvl)
        return rope_toks

    A.push()
    Ke = A.alloc(T)[0:96, :]
    Ko = A.alloc(T)[0:96, :]
    Qe = A.alloc(T)[0:96, :]
    Qo = A.alloc(T)[0:96, :]
    kdst = A.alloc(512)
    Ve = A.alloc(16 * 65)
    Vo = A.alloc(16 * 65)
    PT = A.alloc(2 * 1024)
    wkd_sb = A.alloc(2 * 128)
    wvd_sb = A.alloc(2 * 128)
    wqd_sb = A.alloc(4 * 128)
    oa_sb = A.alloc(T)[0:64, :]
    avf = A.alloc(T)
    zpack = A.alloc(512)
    zrow = zpack
    zrow0 = zpack
    vstage = A.alloc(512)
    zx = vstage[0:64, :]
    head_state = {"oadma": None}
    norm_state = {"dma": None, "mult": None, "bc": None}

    def attn_head(b, h, Kb, Qb, Vb):
        first_oa = [True]
        pending = []

        def flush_norm():
            if not pending:
                return
            sq = pending.pop(0)
            s.waits("tensor", [norm_state["dma"], norm_state["mult"],
                               bank_ev.get(id(psB))])
            tbc = s.emit("tensor", lambda e: e.matmul(psB[0:64, :], o1x64,
                                                      zrow0[0:1, :],
                                                      start=True, stop=True), tok=True)
            s.wait("vector", tbc)
            tzx = s.emit("vector", lambda e: e.tensor_copy(zx, psB[0:64, :]), tok=True)
            bank_ev[id(psB)] = tzx
            tmu = s.emit("vector", lambda e, sq=sq: e.scalar_tensor_tensor(
                oa_sb[0:64, sq * 512:(sq + 1) * 512],
                avf[0:64, sq * 512:(sq + 1) * 512], 1.0, zx,
                ALU.mult, ALU.mult), tok=True)
            norm_state["mult"] = tmu
            norm_state["bc"] = tbc

        for sq4 in range(4):
            nk = 4 * sq4 + 4
            exp_toks = {}
            pv_state = {0: None, 1: None}

            def emit_pv(k0, nk=nk):
                s.wait("tensor", exp_toks[k0])
                grp = (k0 // 2) % 2
                for j in range(2):
                    kj = k0 + j
                    s.emit("tensor", lambda e, kj=kj, grp=grp, nk=nk: e.matmul(
                        psPV[0:65, :],
                        Vb[:, kj * 65: kj * 65 + 65],
                        PT[:, grp * 1024 + (kj % 2) * 512: grp * 1024 + (kj % 2) * 512 + 512],
                        start=(kj == 0), stop=(kj == nk - 1)),
                        tok=(j == 1))
                pv_state[grp] = s.tok("tensor")

            for kk in range(nk):
                slot = kk % 4
                if kk == 3:
                    flush_norm()
                if kk >= 2 and kk % 2 == 0:
                    emit_pv(kk - 2)
                tmm = s.emit("tensor", lambda e, kk=kk, sq4=sq4, slot=slot, Kb=Kb, Qb=Qb:
                             e.matmul(
                    psS[:, slot * 512:(slot + 1) * 512], Kb[:, kk * 128:(kk + 1) * 128],
                    Qb[:, sq4 * 512:(sq4 + 1) * 512], start=True, stop=True), tok=True)
                d = 128 * kk - 512 * sq4
                if d >= 0:
                    s.wait("vector", tmm)
                    tmm = s.emit("vector", lambda e, slot=slot, d=d: e.tensor_tensor(
                        psS[:, slot * 512:(slot + 1) * 512],
                        psS[:, slot * 512:(slot + 1) * 512],
                        mstrip[:, 384 - d: 896 - d], ALU.add), tok=True)
                if kk % 2 == 1:
                    grp2 = (kk // 2) % 2
                    s.wait("scalar", tmm)
                    s.waits("scalar", [pv_state[grp2]])
                    exp_toks[kk - 1] = s.emit("scalar", lambda e, kk=kk, grp2=grp2:
                                              e.activation(
                        out=PT[:, grp2 * 1024:(grp2 + 1) * 1024],
                        in_=psS[:, ((kk - 1) % 4) * 512: ((kk - 1) % 4) * 512 + 1024],
                        func=AF.Exp, scale=SC_ATT), tok=True)
            emit_pv(nk - 2)
            tpv = pv_state[((nk - 2) // 2) % 2]
            s.wait("vector", tpv)
            if first_oa[0]:
                s.waits("vector", [head_state["oadma"]])
                first_oa[0] = False
            tev = s.emit("vector", lambda e, sq4=sq4: e.tensor_copy(
                avf[0:65, sq4 * 512:(sq4 + 1) * 512], psPV[0:65, :]), tok=True)
            s.wait("tensor", tev)
            s.waits("vector", [norm_state["dma"]])
            gapv()
            trc = s.emit("vector", lambda e, sq4=sq4: e.reciprocal(
                zrow[64:65, :], avf[64:65, sq4 * 512:(sq4 + 1) * 512]), tok=True)
            s.waits("gpsimd", [trc, norm_state["bc"]])
            norm_state["dma"] = s.dma("gpsimd", zrow0[0:1, :], zrow[64:65, :], tok=True)
            pending.append(sq4)
        flush_norm()
        tvf = s.tok("vector")
        s.wait("gpsimd", tvf)
        head_state["oadma"] = s.dma("gpsimd", oa_scr[h * 64:(h + 1) * 64, :],
                                    oa_sb[0:64, :], tok=True)

    def head_pair(b, p):
        s.wait("sync", s.tok("tensor"))
        twk = s.dma("sync", _sview(wkd_sb, 2, 128), _wview(io["wkdec"][:, p * 128:(p + 1) * 128], 2, 128))
        twv = s.dma("sync", _sview(wvd_sb, 2, 128), _wview(io["wvdec"][:, p * 128:(p + 1) * 128], 2, 128))
        twq = s.dma("sync", _sview(wqd_sb, 4, 128), _wview(io["wqdec"][:, p * 128:(p + 1) * 128], 4, 128))
        s.waits("tensor", [twk, twv, twq])
        shift_toks = []
        prev_kdma = g2.get(("rst",))
        for tt4 in range(4):
            tsl = slice(tt4 * 512, (tt4 + 1) * 512)
            for ch in range(2):
                s.emit("tensor", lambda e, ch=ch, tsl=tsl: e.matmul(
                    psA, wkd_sb[:, ch * 128:(ch + 1) * 128], kvc[:, ch * T:][:, tsl],
                    start=(ch == 0), stop=(ch == 1)), tok=(ch == 1))
            tmm = s.tok("tensor")
            s.wait("vector", tmm)
            s.emit("vector", lambda e, tsl=tsl: e.tensor_copy(Ke[0:64, tsl], psA[0:64, :]))
            s.waits("vector", [prev_kdma])
            tv = s.emit("vector", lambda e: e.tensor_copy(kdst, psA), tok=True)
            s.wait("gpsimd", tv)
            prev_kdma = s.dma("gpsimd", Ko[0:64, tsl], kdst[64:128, :], tok=True)
            shift_toks.append(prev_kdma)
            s.wait("tensor", tv)
            for ch in range(4):
                s.emit("tensor", lambda e, ch=ch, tsl=tsl: e.matmul(
                    psA, wqd_sb[:, ch * 128:(ch + 1) * 128], qc[:, ch * T:][:, tsl],
                    start=(ch == 0), stop=(ch == 3)), tok=(ch == 3))
            tmm = s.tok("tensor")
            s.wait("vector", tmm)
            s.emit("vector", lambda e, tsl=tsl: e.tensor_copy(Qe[0:64, tsl], psA[0:64, :]))
            s.waits("vector", [prev_kdma])
            tv = s.emit("vector", lambda e: e.tensor_copy(kdst, psA), tok=True)
            s.wait("gpsimd", tv)
            prev_kdma = s.dma("gpsimd", Qo[0:64, tsl], kdst[64:128, :], tok=True)
            shift_toks.append(prev_kdma)
            s.wait("tensor", tv)
            for ch in range(2):
                s.emit("tensor", lambda e, ch=ch, tsl=tsl: e.matmul(
                    psA, wvd_sb[:, ch * 128:(ch + 1) * 128], kvc[:, ch * T:][:, tsl],
                    start=(ch == 0), stop=(ch == 1)), tok=(ch == 1))
            tmm = s.tok("tensor")
            s.wait("vector", tmm)
            tv = s.emit("vector", lambda e: e.tensor_copy(vstage, psA), tok=True)
            s.wait("tensor", tv)
            for k4 in range(4):
                kk = tt4 * 4 + k4
                ttk = transpose(psT[0:128, 0:128], vstage[:, k4 * 128:(k4 + 1) * 128],
                                tok=True)
                s.wait("vector", ttk)
                s.emit("vector", lambda e, kk=kk: e.tensor_copy(
                    Ve[:, kk * 65: kk * 65 + 64], psT[:, 0:64]))
                tv = s.emit("vector", lambda e, kk=kk: e.tensor_copy(
                    Vo[:, kk * 65: kk * 65 + 64], psT[:, 64:128]), tok=True)
                s.wait("tensor", tv)
        s.wait("gpsimd", s.tok("tensor"))
        for h, Kb2, Qb2 in ((2 * p, Ke, Qe), (2 * p + 1, Ko, Qo)):
            r0 = (h // 4) * 128 + (h % 4) * 32
            shift_toks.append(s.dma("gpsimd", Kb2[64:96, :], kro_scr[r0:r0 + 32, :],
                                    tok=True))
            shift_toks.append(s.dma("gpsimd", Qb2[64:96, :], qro_scr[r0:r0 + 32, :],
                                    tok=True))
        s.emit("vector", lambda e: e.memset(
            _r(Ve, "p (k w) -> p k w", k=16, w=65)[:, :, 64:65], 1.0))
        tvm = s.emit("vector", lambda e: e.memset(
            _r(Vo, "p (k w) -> p k w", k=16, w=65)[:, :, 64:65], 1.0), tok=True)
        s.waits("tensor", shift_toks + [tvm])
        if p == 0 and b == 1 and "Ke" in taps:
            s.waits("gpsimd", shift_toks + [tvm])
            s.dma("gpsimd", tap_aps["Ke"], Ke, tok=True)
            s.dma("gpsimd", tap_aps["Qe"], Qe, tok=True)
            s.dma("gpsimd", tap_aps["Ve"], Ve, tok=True)
        attn_head(b, 2 * p, Ke, Qe, Ve)
        attn_head(b, 2 * p + 1, Ko, Qo, Vo)

    wo_sb = [A.alloc(512), A.alloc(512)]
    xres = [A.alloc(C), A.alloc(C)]
    oaT = [A.alloc(128), A.alloc(128)]
    rst = kdst

    def wo_phase(b):
        tk_oa = s.dtok("gpsimd")
        s.wait("sync", tk_oa)
        s.wait("sync", s.tok("tensor"))
        prev_rdma = None
        wg = {}
        for tc in range(16):
            bufx = tc % 2
            s.waits("sync", [wg.get(("xres", bufx))])
            tk_x = s.dma("sync", xres[bufx],
                         io["xa"][b, tc * 128:(tc + 1) * 128, :])
            for chalf in range(2):
                pb = psA if chalf == 0 else psB
                tmm = None
                for ch in range(8):
                    bufw = ch % 2
                    s.waits("sync", [wg.get(("wo", bufw))])
                    tk_w = s.dma("sync", wo_sb[bufw],
                                 io["wo"][ch * 128:(ch + 1) * 128,
                                          chalf * 512:(chalf + 1) * 512])
                    tk_o = s.dma("sync", oaT[bufw],
                                 oa_scr[ch * 128:(ch + 1) * 128,
                                        tc * 128:(tc + 1) * 128])
                    s.waits("tensor", [tk_w, tk_o])
                    tpe = s.emit("tensor", lambda e, pb=pb, bufw=bufw, ch=ch: e.matmul(
                        pb, oaT[bufw], wo_sb[bufw], start=(ch == 0), stop=(ch == 7)),
                        tok=True)
                    wg[("wo", bufw)] = tpe
                tmm = s.tok("tensor")
                s.waits("vector", [tmm, tk_x, prev_rdma])
                tv = s.emit("vector", lambda e, pb=pb, bufx=bufx, chalf=chalf:
                            e.tensor_tensor(rst, pb,
                                            xres[bufx][:, chalf * 512:(chalf + 1) * 512],
                                            ALU.add), tok=True)
                s.wait("tensor", tv)
                s.wait("gpsimd", tv)
                if chalf == 1:
                    wg[("xres", bufx)] = tv
                prev_rdma = s.dma(
                    "gpsimd",
                    cc_ra_in.ap()[b * T + tc * 128: b * T + (tc + 1) * 128,
                                  chalf * 512:(chalf + 1) * 512],
                    rst, tok=True)
                g2[("rst",)] = prev_rdma

    tap("Ke", (96, T)); tap("Qe", (96, T)); tap("Ve", (128, 16 * 65))
    tk_ccra = None
    if upto in ("p2", "all"):
        for b in range(BPC):
            rt = a_phase(b)
            s.waits("tensor", rt)
            for p in range(8):
                head_pair(b, p)
            wo_phase(b)
        s.wait("gpsimd", s.dtok("gpsimd"))
        tk_ccra = s.collective("AllGather", ALU.bypass,
                               [cc_ra_in.ap().opt()], [cc_ra_out.ap().opt()])
        if tap("kvc", (128, 2 * T)) is not None:
            tv = s.tok("vector")
            s.wait("gpsimd", tv)
            s.dma("gpsimd", tap_aps["kvc"], kvc, tok=True)
        if tap("oascr", (C, T)) is not None:
            s.dma("gpsimd", tap_aps["oascr"], oa_scr, tok=True)
        if tap("rain", (BPC * T, C)) is not None:
            s.wait("gpsimd", tk_ccra)
            s.dma("gpsimd", tap_aps["rain"], cc_ra_in.ap(), tok=True)
    A.pop()
    A.pop()

    A.push()
    attF = A.alloc(2048)[0:16, :]
    attL = A.alloc(256)[0:16, :]
    am = A.alloc(1)[0:16, :]
    z16 = A.alloc(1)[0:16, :]
    zi16 = A.alloc(1)[0:16, :]
    pmy = A.alloc(256)[0:16, :]
    pfl = A.alloc(FH * JPC)[0:1, :]
    pbc = A.alloc(FH * JPC)[0:64, :]
    av = A.alloc(FH * JPC)[0:64, :]
    fwo_sb = [A.alloc(C)[0:64, :], A.alloc(C)[0:64, :]]
    g3 = {}
    aost = A.alloc(1024)
    ssc_sb = [A.alloc(128), A.alloc(128)]
    aot_sb = [A.alloc(C), A.alloc(C)]
    hti = [A.alloc(C), A.alloc(C)]
    ysb = A.alloc(C)
    ynb = A.alloc(C)
    rat = [A.alloc(C), A.alloc(C)]
    st3 = {k: A.alloc(1) for k in ("m", "ss", "v", "r", "t0", "t1")}
    st3["sq"] = A.alloc(C)

    if upto == "all":
        s.wait("tensor", s.tok("vector"))
        s.wait("sync", s.dtok("gpsimd"))
        s.wait("sync", s.tok("vector"))
        s.wait("sync", s.tok("tensor"))
        s.wait("gpsimd", s.tok("vector"))
        s.wait("gpsimd", tk_ccatt)
        tk_af = s.dma("gpsimd", _r(attF, "p (r j) -> p r j", r=NCORE, j=JPC),
                      _r(cc_att_out.ap(), "r (g j) -> g r j", g=FH, j=JPC), tok=True)
        tk_al = None
        s.waits("vector", [tk_af])
        tv = s.emit("vector", lambda e: e.reduce_max(am, attF, axis=AX.X, negate=True),
                    tok=True)
        s.wait("gpsimd", tv)
        tg = s.emit("gpsimd", lambda e: e.tensor_scalar(am, am, SC_FFN, None, ALU.mult),
                    tok=True)
        s.waits("scalar", [tg])
        ta = s.emit("scalar", lambda e: e.activation(out=attF, in_=attF, func=AF.Exp,
                                                     scale=SC_FFN, bias=am,
                                                     accum_out=z16), tok=True)
        ta2 = s.emit("scalar", lambda e: e.activation(out=pmy, in_=attL_p, func=AF.Exp,
                                                      scale=SC_FFN, bias=am), tok=True)
        s.waits("vector", [ta, ta2])
        s.emit("vector", lambda e: e.reciprocal(zi16, z16))
        gapv()
        tv = s.emit("vector", lambda e: e.tensor_scalar(pmy, pmy, zi16, None, ALU.mult),
                    tok=True)
        s.wait("gpsimd", tv)
        tk_p1 = s.dma("gpsimd", p_scr, pmy, tok=True)
        s.wait("gpsimd", tk_p1)
        tk_p2 = s.dma("gpsimd", pfl, _r(p_scr, "g j -> (g j)").unsqueeze(0), tok=True)
        s.wait("tensor", tk_p2)
        for i8 in range(8):
            tmm = s.emit("tensor", lambda e, i8=i8: e.matmul(
                psB[0:64, :], o1x64, pfl[:, i8 * 512:(i8 + 1) * 512],
                start=True, stop=True), tok=True)
            s.wait("vector", tmm)
            tv = s.emit("vector", lambda e, i8=i8: e.tensor_copy(
                pbc[:, i8 * 512:(i8 + 1) * 512], psB[0:64, :]), tok=True)
            s.wait("tensor", tv)
        tvav = s.emit("vector", lambda e: e.tensor_tensor(av, vall[0:64, :], pbc,
                                                          ALU.mult), tok=True)
        prev_adma = None
        for g in range(FH):
            bufw = g % 2
            s.waits("sync", [g3.get(("fwo", bufw))])
            tk_w = s.dma("sync", fwo_sb[bufw], io["fwo"][g])
            s.waits("tensor", [tk_w, tvav])
            for jh in range(2):
                tmm = None
                for chalf in range(2):
                    pb = psA if chalf == 0 else psB
                    s.emit("tensor", lambda e, g=g, jh=jh, chalf=chalf, pb=pb, bufw=bufw:
                           e.matmul(pb,
                                    av[:, g * 256 + jh * 128: g * 256 + jh * 128 + 128],
                                    fwo_sb[bufw][:, chalf * 512:(chalf + 1) * 512],
                                    start=True, stop=True), tok=(chalf == 1))
                tmm = s.tok("tensor")
                g3[("fwo", bufw)] = tmm
                s.waits("vector", [tmm, prev_adma])
                tv2 = None
                for chalf in range(2):
                    pb = psA if chalf == 0 else psB
                    tv2 = s.emit("vector", lambda e, pb=pb, chalf=chalf: e.tensor_tensor(
                        aost[:, chalf * 512:(chalf + 1) * 512], pb,
                        bob[:, chalf * 512:(chalf + 1) * 512], ALU.add),
                        tok=(chalf == 1))
                s.wait("tensor", tv2)
                s.wait("scalar", tv2)
                dstv = _r(ao_scr, "(j g) c -> j g c", j=JPC, g=FH)[
                    jh * 128:(jh + 1) * 128, g:g + 1, :]
                prev_adma = s.dma("scalar", dstv, aost.unsqueeze(1), tok=True)
        tk_ao = s.dtok("scalar")

        s.wait("tensor", s.tok("vector"))
        s.wait("sync", tk_ao)
        s.wait("gpsimd", tk_ccra)
        prev_odma = None
        prev_yadd = None
        dbg_toks = {"y": None, "f": None}
        ra_flat = cc_ra_out.ap()
        for t in range(32):
            bt = t % 2
            s.waits("sync", [g3.get(("tile", bt))])
            tk_s = s.dma("sync", ssc_sb[bt], ssc_scr[t])
            tk_a = s.dma("sync", aot_sb[bt], ao_scr[t * 128:(t + 1) * 128, :])
            tk_h = s.dma("sync", hti[bt], h_scr[t * 128:(t + 1) * 128, :])
            s.waits("gpsimd", [g3.get(("rat", bt))])
            tk_r = s.idma(rat[bt], ra_flat, ridx_sb[:, t:t + 1])
            s.waits("tensor", [tk_s, tk_a, prev_yadd])
            for chalf in range(2):
                pb = psA if chalf == 0 else psB
                s.emit("tensor", lambda e, bt=bt, chalf=chalf, pb=pb: e.matmul(
                    pb, ssc_sb[bt], aot_sb[bt][:, chalf * 512:(chalf + 1) * 512],
                    start=True, stop=True), tok=(chalf == 1))
            tmm = s.tok("tensor")
            s.waits("vector", [tmm, tk_h])
            tv = None
            s.waits("vector", [dbg_toks["y"]])
            for chalf in range(2):
                pb = psA if chalf == 0 else psB
                tv = s.emit("vector", lambda e, bt=bt, chalf=chalf, pb=pb: e.tensor_tensor(
                    ysb[:, chalf * 512:(chalf + 1) * 512], pb,
                    hti[bt][:, chalf * 512:(chalf + 1) * 512], ALU.add),
                    tok=(chalf == 1))
            s.wait("tensor", tv)
            prev_yadd = tv
            g3[("tile", bt)] = tv
            if "dbgy" in taps:
                s.wait("gpsimd", tv)
                dbg_toks["y"] = s.dma("gpsimd", dbg_y[t * 128:(t + 1) * 128, :], ysb,
                                      tok=True)
            s.waits("vector", [prev_odma, dbg_toks["f"]])
            tk_n = layernorm(ysb, ynb, st3, C, eps_ln)
            s.waits("vector", [tk_n, tk_r])
            s.emit("vector", lambda e: e.tensor_tensor(ynb, ynb, lngb, ALU.mult))
            tfa = s.emit("vector", lambda e: e.tensor_tensor(ynb, ynb, lnbb, ALU.add),
                         tok=True)
            if "dbgy" in taps:
                s.wait("gpsimd", tfa)
                dbg_toks["f"] = s.dma("gpsimd", dbg_f[t * 128:(t + 1) * 128, :], ynb,
                                      tok=True)
            tv = s.emit("vector", lambda e, bt=bt: e.tensor_tensor(ynb, ynb, rat[bt],
                                                                   ALU.add), tok=True)
            g3[("rat", bt)] = tv
            s.wait("scalar", tv)
            outT = _r(out, "i j c -> j i c")
            for jr in range(8):
                prev_odma = s.dma("scalar", outT[t * 8 + jr],
                                  ynb[jr * 16:(jr + 1) * 16, :], tok=True)
        if tap("pmy", (16, JPC)) is not None:
            tv = s.tok("vector")
            s.wait("gpsimd", tv)
            s.dma("gpsimd", tap_aps["pmy"], pmy, tok=True)
        if tap("av", (64, FH * JPC)) is not None:
            s.dma("gpsimd", tap_aps["av"], av, tok=True)
        if tap("aoscr", (FROWS, C)) is not None:
            s.dma("gpsimd", tap_aps["aoscr"], ao_scr, tok=True)
        if tap("attf", (16, 2048)) is not None:
            s.dma("gpsimd", tap_aps["attf"], attF, tok=True)
        if tap("stats", (16, 4)) is not None:
            s.dma("gpsimd", tap_aps["stats"][:, 0:1], am, tok=True,
                  allow_slow_non_contiguous=True)
            s.dma("gpsimd", tap_aps["stats"][:, 1:2], z16, tok=True,
                  allow_slow_non_contiguous=True)
            s.dma("gpsimd", tap_aps["stats"][:, 2:3], zi16, tok=True,
                  allow_slow_non_contiguous=True)
        if tap("pmyt", (16, JPC)) is not None:
            s.dma("gpsimd", tap_aps["pmyt"], pmy, tok=True)
        if tap("ssc", (32 * 128, 128)) is not None:
            s.dma("gpsimd", tap_aps["ssc"], _r(ssc_scr, "t p c -> (t p) c"), tok=True)
        if tap("dbgy", (FROWS, C)) is not None:
            s.dma("gpsimd", tap_aps["dbgy"], dbg_y, tok=True)
        if tap("dbgf", (FROWS, C)) is not None:
            s.dma("gpsimd", tap_aps["dbgf"], dbg_f, tok=True)
        s.wait("gpsimd", s.dtok("gpsimd"))
        s.wait("gpsimd", s.dtok("scalar"))
    A.pop()

    with nc.Block() as block:
        @block.sync
        def _(eng):
            for f in s.ops["sync"]:
                f(eng)
        @block.tensor
        def _(eng):
            for f in s.ops["tensor"]:
                f(eng)
        @block.vector
        def _(eng):
            for f in s.ops["vector"]:
                f(eng)
        @block.scalar
        def _(eng):
            for f in s.ops["scalar"]:
                f(eng)
        @block.gpsimd
        def _(eng):
            for f in s.ops["gpsimd"]:
                f(eng)

    return nc, sorted(tap_aps)


def make_in_maps(inputs):
    consts = host_constants()
    x = np.ascontiguousarray(inputs["L0_attn_x"], dtype=np.float32)
    in_maps = []
    for r in range(NCORE):
        m = {
            "xa": np.ascontiguousarray(x[2 * r:2 * r + 2]),
            "xf": np.ascontiguousarray(
                x[:, r * JPC:(r + 1) * JPC, :].transpose(1, 0, 2).reshape(FROWS, C)),
            "wkv": inputs["L0_attn_W_kv"], "gkv": inputs["L0_attn_g_kv"],
            "wkdec": inputs["L0_attn_W_kdec"], "wvdec": inputs["L0_attn_W_vdec"],
            "wq": inputs["L0_attn_W_q"], "wqdec": inputs["L0_attn_W_qdec"],
            "wkrope": inputs["L0_attn_W_krope"], "wqrope": inputs["L0_attn_W_qrope"],
            "wo": inputs["L0_attn_W_o"],
            "fwg": inputs["L0_ffn_Wg"], "fwq": inputs["L0_ffn_Wq"],
            "fbqT": np.ascontiguousarray(np.asarray(inputs["L0_ffn_bq"]).T),
            "fwkv": inputs["L0_ffn_Wkv"],
            "fbkvT": np.ascontiguousarray(np.asarray(inputs["L0_ffn_bkv"]).T),
            "fwo": inputs["L0_ffn_Wo"], "fbo": inputs["L0_ffn_bo"],
            "flng": inputs["L0_ffn_ln_g"], "flnb": inputs["L0_ffn_ln_b"],
            "ridx": host_ridx(r),
        }
        m.update(consts)
        in_maps.append({k: np.ascontiguousarray(np.asarray(v, dtype=np.uint32)
                                                if k == "ridx"
                                                else np.asarray(v, dtype=np.float32))
                        for k, v in m.items()})
    return in_maps


def kernel(**inputs):
    nc, _ = build_program("all")
    in_maps = make_in_maps(inputs)
    res = run_bass_kernel_spmd(nc, in_maps, core_ids=list(range(NCORE)))
    shards = [res.results[r]["out"] for r in range(NCORE)]
    return np.ascontiguousarray(np.concatenate(shards, axis=1))


# version 24
# speedup vs baseline: 1.2840x; 1.2111x over previous
import os
os.environ.setdefault("TRNINF_ENABLE_CUSTOMCOMMS_RDH_AG", "1")
os.environ.setdefault("TRNINF_ENABLE_CUSTOMCOMMS_RDH_AR", "1")

import math
import numpy as np

import concourse.bass as bass
import concourse.mybir as mybir
from concourse.bass_utils import run_bass_kernel_spmd

F32 = mybir.dt.float32
U32 = mybir.dt.uint32
ALU = mybir.AluOpType
AF = mybir.ActivationFunctionType
AX = mybir.AxisListType

NCORE = 8
B, T, C = 16, 2048, 1024
H, HD, RD = 16, 64, 32
KVR, QR = 256, 512
BPC = B // NCORE
JPC = T // NCORE
FROWS = B * JPC
FH, FDH = 16, 64
SC_ATT = 1.0 / math.sqrt(HD + RD)
SC_FFN = 1.0 / math.sqrt(FDH)

ENGINES = ("sync", "tensor", "vector", "scalar", "gpsimd")


class Sched:
    def __init__(self, nc):
        self.nc = nc
        self.ops = {e: [] for e in ENGINES}
        self.sem = {e: nc.alloc_semaphore(f"sem_{e}") for e in ENGINES}
        self.cnt = {e: 0 for e in ENGINES}
        self.dsem = {e: nc.alloc_semaphore(f"dsem_{e}") for e in ("sync", "gpsimd", "scalar")}
        self.dcnt = {e: 0 for e in ("sync", "gpsimd", "scalar")}
        self.ccsem = nc.alloc_semaphore("sem_cc")
        self.cccnt = 0

    def emit(self, eng, fn, tok=False):
        if tok:
            self.cnt[eng] += 1
            sem, val = self.sem[eng], self.cnt[eng]
            def wrapped(e, fn=fn, sem=sem):
                fn(e).then_inc(sem, 1)
            self.ops[eng].append(wrapped)
            return (eng, "c", val)
        self.ops[eng].append(fn)
        return None

    def tok(self, eng):
        return (eng, "c", self.cnt[eng])

    def dtok(self, eng):
        return (eng, "d", self.dcnt[eng])

    def dma(self, eng, out, in_, tok=True, **kw):
        if tok:
            self.dcnt[eng] += 16
            sem, val = self.dsem[eng], self.dcnt[eng]
            def wrapped(e, out=out, in_=in_, sem=sem, kw=kw):
                e.dma_start(out=out, in_=in_, **kw).then_inc(sem, 16)
            self.ops[eng].append(wrapped)
            return (eng, "d", val)
        def plain(e, out=out, in_=in_, kw=kw):
            e.dma_start(out=out, in_=in_, **kw)
        self.ops[eng].append(plain)
        return None

    def idma(self, out, in_, idx_ap):
        self.dcnt["gpsimd"] += 16
        sem, val = self.dsem["gpsimd"], self.dcnt["gpsimd"]
        def wrapped(e, out=out, in_=in_, idx_ap=idx_ap, sem=sem):
            e.indirect_dma_start(
                out=out, out_offset=None, in_=in_,
                in_offset=bass.IndirectOffsetOnAxis(ap=idx_ap, axis=0),
            ).then_inc(sem, 16)
        self.ops["gpsimd"].append(wrapped)
        return ("gpsimd", "d", val)

    def collective(self, kind, op, ins, outs):
        self.cccnt += 1
        val = self.cccnt
        sem = self.ccsem
        def fn(e, kind=kind, op=op, ins=ins, outs=outs, sem=sem):
            e.collective_compute(
                kind, op, replica_groups=[list(range(NCORE))],
                ins=ins, outs=outs,
            ).then_inc(sem, 1)
        self.ops["gpsimd"].append(fn)
        return ("cc", "c", val)

    def wait(self, eng, tk):
        if tk is None:
            return
        src, kind, val = tk
        sem = self.ccsem if src == "cc" else (self.dsem[src] if kind == "d" else self.sem[src])
        self.ops[eng].append(lambda e, sem=sem, val=val: e.wait_ge(sem, val))

    def waits(self, eng, toks):
        best = {}
        for t in toks:
            if t is None:
                continue
            k = (t[0], t[1])
            if k not in best or best[k][2] < t[2]:
                best[k] = t
        for t in best.values():
            self.wait(eng, t)


class Arena:
    def __init__(self, nc, words):
        self.t = nc.alloc_sbuf_tensor("arena", [128, words], F32).ap()
        self.words = words
        self.off = 0
        self.marks = []

    def alloc(self, words, align=128):
        off = -(-self.off // align) * align
        assert off + words <= self.words, f"arena overflow {off}+{words}>{self.words}"
        self.off = off + words
        return self.t[:, off:off + words]

    def push(self):
        self.marks.append(self.off)

    def pop(self):
        self.off = self.marks.pop()


def host_constants():
    c = {}
    c["ident"] = np.eye(128, dtype=np.float32)
    g = np.arange(896)[None, :]
    p = np.arange(128)[:, None]
    c["mstrip"] = np.where(g < p + 384, -1e30, 0.0).astype(np.float32)
    inv_freq = 1.0 / 10000.0 ** (np.arange(0, RD, 2, dtype=np.float32) / RD)
    t = np.arange(T, dtype=np.float32)
    freqs = np.outer(t, inv_freq)
    emb = np.concatenate([freqs, freqs], -1)
    c["coss"] = np.tile(np.cos(emb).T, (4, 1)).astype(np.float32)
    c["sins"] = np.tile(np.sin(emb).T, (4, 1)).astype(np.float32)
    P = np.zeros((128, 128), dtype=np.float32)
    for blk in range(4):
        b0 = blk * 32
        for d in range(16):
            P[b0 + d + 16, b0 + d] = -1.0
            P[b0 + d, b0 + d + 16] = 1.0
    c["pmat"] = P
    repj = np.zeros((8, 128), dtype=np.float32)
    for j in range(8):
        repj[j, j * 16:(j + 1) * 16] = 1.0
    c["repj"] = repj
    reg = np.zeros((16, 128), dtype=np.float32)
    for j in range(8):
        for g in range(16):
            reg[g, j * 16 + g] = 1.0
    c["reg"] = reg
    mj = np.zeros((128, 128), dtype=np.float32)
    for j in range(8):
        mj[j * 16:(j + 1) * 16, j * 16:(j + 1) * 16] = 1.0
    c["mj"] = mj
    c["ones_1x64"] = np.ones((1, 64), dtype=np.float32)
    c["ones_1x128"] = np.ones((1, 128), dtype=np.float32)
    c["ones_128x1"] = np.ones((128, 1), dtype=np.float32)
    return c


def host_ridx(rank):
    idx = np.zeros((128, 32), dtype=np.uint32)
    for t in range(32):
        for jr in range(8):
            for i in range(16):
                idx[jr * 16 + i, t] = i * T + rank * JPC + 8 * t + jr
    return idx


def _r(ap, spec, **kw):
    return ap.rearrange(spec, **kw)


def _wview(hbm_ap, a, b):
    return _r(hbm_ap, "(a b) c -> b a c", a=a, b=b)


def _sview(sb_ap, a, c):
    return _r(sb_ap, "p (a c) -> p a c", a=a, c=c)


def build_program(upto="all", taps=()):
    taps = set(taps)
    nc = bass.Bass("TRN2", target_bir_lowering=False)

    def din(name, shape, dt=F32):
        return nc.dram_tensor(name, list(shape), dt, kind="ExternalInput").ap()

    io = {}
    io["xa"] = din("xa", (BPC, T, C))
    io["xf"] = din("xf", (FROWS, C))
    for n, sh in [("wkv", (C, KVR)), ("gkv", (KVR,)), ("wkdec", (KVR, C)),
                  ("wvdec", (KVR, C)), ("wq", (C, QR)), ("wqdec", (QR, C)),
                  ("wkrope", (C, H * RD)), ("wqrope", (QR, H * RD)), ("wo", (C, C)),
                  ("fwg", (C, FH)), ("fwq", (FH, C, FDH)), ("fbqT", (FDH, FH)),
                  ("fwkv", (FH, C, 2 * FDH)), ("fbkvT", (2 * FDH, FH)),
                  ("fwo", (FH, FDH, C)), ("fbo", (C,)), ("flng", (C,)), ("flnb", (C,)),
                  ("ident", (128, 128)), ("mstrip", (128, 896)), ("coss", (128, T)),
                  ("sins", (128, T)), ("pmat", (128, 128)), ("ones_1x64", (1, 64)),
                  ("ones_1x128", (1, 128)), ("ones_128x1", (128, 1)),
                  ("repj", (8, 128)), ("reg", (16, 128)), ("mj", (128, 128))]:
        io[n] = din(n, sh)
    io["ridx"] = din("ridx", (128, 32), U32)

    out = nc.dram_tensor("out", [B, JPC, C], F32, kind="ExternalOutput").ap()

    h_scr = nc.dram_tensor("h_scr", [FROWS, C], F32).ap()
    ssc_scr = nc.dram_tensor("ssc_scr", [32, 128, 128], F32).ap()
    kro_scr = nc.dram_tensor("kro_scr", [H * RD, T], F32).ap()
    qro_scr = nc.dram_tensor("qro_scr", [H * RD, T], F32).ap()
    oa_scr = nc.dram_tensor("oa_scr", [C, T], F32).ap()
    ao_scr = nc.dram_tensor("ao_scr", [FROWS, C], F32).ap()
    p_scr = nc.dram_tensor("p_scr", [FH, JPC], F32).ap()
    dbg_y = nc.dram_tensor("dbg_y", [FROWS, C], F32).ap()
    dbg_f = nc.dram_tensor("dbg_f", [FROWS, C], F32).ap()
    cc_att_in = nc.dram_tensor("cc_att_in", [1, FH * JPC], F32)
    cc_att_out = nc.dram_tensor("cc_att_out", [NCORE, FH * JPC], F32, addr_space="Shared")
    cc_ra_in = nc.dram_tensor("cc_ra_in", [BPC * T, C], F32)
    cc_ra_out = nc.dram_tensor("cc_ra_out", [NCORE * BPC * T, C], F32, addr_space="Shared")

    tap_aps = {}
    def tap(name, shape):
        if name in taps:
            tap_aps[name] = nc.dram_tensor("tap_" + name, list(shape), F32,
                                           kind="ExternalOutput").ap()
        return tap_aps.get(name)

    s = Sched(nc)
    A = Arena(nc, 50944)

    ident = A.alloc(128)
    mstrip = A.alloc(896)
    pmat = A.alloc(128)
    coss = A.alloc(T)
    sins = A.alloc(T)
    o1x64 = A.alloc(64)[0:1, :]
    o1x128 = A.alloc(128)[0:1, :]
    o128x1 = A.alloc(1)
    gkv_sb = A.alloc(2)
    fbqT_sb = A.alloc(FH)
    fbkvT_sb = A.alloc(FH)
    lngb = A.alloc(C)
    lnbb = A.alloc(C)
    bob = A.alloc(C)
    eps_ln = A.alloc(1)
    eps_rms = A.alloc(1)
    gap1 = A.alloc(8)
    repj_sb = A.alloc(128)[0:8, :]
    reg_sb = A.alloc(128)[0:16, :]
    mj_sb = A.alloc(128)
    ridx_sb = A.alloc(32).bitcast(U32)
    attL_p = A.alloc(JPC)[0:16, :]
    vall = A.alloc(FH * JPC)

    psS = nc.alloc_psum_tensor("psS", [128, 2048], F32).ap()
    psPV = nc.alloc_psum_tensor("psPV", [128, 512], F32).ap()
    psA = nc.alloc_psum_tensor("psA", [128, 512], F32).ap()
    psB = nc.alloc_psum_tensor("psB", [128, 512], F32).ap()
    psT = nc.alloc_psum_tensor("psT", [128, 512], F32).ap()

    ctoks = [
        s.dma("sync", ident, io["ident"]),
        s.dma("sync", mstrip, io["mstrip"]),
        s.dma("sync", pmat, io["pmat"]),
        s.dma("sync", coss, io["coss"]),
        s.dma("sync", sins, io["sins"]),
        s.dma("sync", o1x64, io["ones_1x64"]),
        s.dma("sync", o1x128, io["ones_1x128"]),
        s.dma("sync", o128x1, io["ones_128x1"]),
        s.dma("sync", gkv_sb, _r(io["gkv"], "(a b) -> b a", a=2, b=128),
              allow_slow_non_contiguous=True),
        s.dma("sync", fbqT_sb[0:FDH, :], io["fbqT"]),
        s.dma("sync", fbkvT_sb, io["fbkvT"]),
        s.dma("sync", ridx_sb, io["ridx"]),
        s.dma("sync", repj_sb, io["repj"]),
        s.dma("sync", reg_sb, io["reg"]),
        s.dma("sync", mj_sb, io["mj"]),
    ]

    A.push()
    row3 = A.alloc(3 * C)
    r_flng, r_flnb, r_fbo = row3[0:1, 0:C], row3[0:1, C:2 * C], row3[0:1, 2 * C:3 * C]
    ctoks.append(s.dma("sync", r_flng, io["flng"].unsqueeze(0)))
    ctoks.append(s.dma("sync", r_flnb, io["flnb"].unsqueeze(0)))
    ctoks.append(s.dma("sync", r_fbo, io["fbo"].unsqueeze(0)))
    s.waits("tensor", ctoks)
    for row, dst in ((r_flng, lngb), (r_flnb, lnbb), (r_fbo, bob)):
        for hf in range(2):
            tmm = s.emit("tensor", lambda e, row=row, hf=hf: e.matmul(
                psB, o1x128, row[:, hf * 512:(hf + 1) * 512], start=True, stop=True),
                tok=True)
            s.wait("vector", tmm)
            tv = s.emit("vector", lambda e, dst=dst, hf=hf: e.tensor_copy(
                dst[:, hf * 512:(hf + 1) * 512], psB), tok=True)
            s.wait("tensor", tv)
    s.emit("vector", lambda e: e.memset(eps_ln, 1e-5))
    s.emit("vector", lambda e: e.memset(eps_rms, 1e-6))
    s.emit("vector", lambda e: e.memset(gap1, 0.0))
    s.wait("sync", s.tok("tensor"))
    A.pop()

    def gapv():
        s.emit("vector", lambda e: e.tensor_copy(gap1[:, 0:4], gap1[:, 4:8]))

    def rsqrt_chain(v_ap, r_ap, t0_ap, t1_ap, pre_tok):
        s.wait("scalar", pre_tok)
        s.emit("scalar", lambda e: e.activation(out=r_ap, in_=v_ap, func=AF.Ln))
        ta = s.emit("scalar", lambda e: e.activation(out=r_ap, in_=r_ap,
                                                     func=AF.Exp, scale=-0.5), tok=True)
        s.wait("gpsimd", ta)
        s.emit("gpsimd", lambda e: e.tensor_tensor(t0_ap, r_ap, r_ap, ALU.mult))
        s.emit("gpsimd", lambda e: e.tensor_tensor(t1_ap, v_ap, t0_ap, ALU.mult))
        s.emit("gpsimd", lambda e: e.tensor_scalar(t1_ap, t1_ap, -0.5, 1.5,
                                                   ALU.mult, op1=ALU.add))
        return s.emit("gpsimd", lambda e: e.tensor_tensor(r_ap, r_ap, t1_ap,
                                                          ALU.mult), tok=True)

    def layernorm(xin, xout, st, D, eps_tile):
        s.emit("vector", lambda e: e.tensor_scalar(st["sq"], xin, 1.0, 0.0,
                                                   ALU.mult, op1=ALU.add,
                                                   accum_out=st["m"]))
        s.emit("vector", lambda e: e.tensor_tensor(st["sq"], xin, xin, ALU.mult))
        tv = s.emit("vector", lambda e: e.tensor_scalar(st["sq"], st["sq"], 1.0, 0.0,
                                                        ALU.mult, op1=ALU.add,
                                                        accum_out=st["ss"]), tok=True)
        s.wait("gpsimd", tv)
        s.emit("gpsimd", lambda e: e.tensor_scalar(st["m"], st["m"], 1.0 / D, None, ALU.mult))
        s.emit("gpsimd", lambda e: e.tensor_scalar(st["v"], st["ss"], 1.0 / D, None, ALU.mult))
        s.emit("gpsimd", lambda e: e.tensor_tensor(st["t0"], st["m"], st["m"], ALU.mult))
        s.emit("gpsimd", lambda e: e.tensor_tensor(st["v"], st["v"], st["t0"], ALU.subtract))
        tg = s.emit("gpsimd", lambda e: e.tensor_scalar(st["v"], st["v"], eps_tile,
                                                        None, ALU.add), tok=True)
        tn = rsqrt_chain(st["v"], st["r"], st["t0"], st["t1"], tg)
        s.wait("vector", tn)
        return s.emit("vector", lambda e: e.tensor_scalar(xout, xin, st["m"], st["r"],
                                                          ALU.subtract, op1=ALU.mult),
                      tok=True)

    def transpose(dst_psum, src, tok=False):
        p = src.partition_size()
        idn = ident if p == 128 else ident[0:p, 0:p]
        return s.emit("tensor", lambda e, dst_psum=dst_psum, src=src, idn=idn:
                      e.transpose(dst_psum, src, idn), tok=tok)

    A.push()
    xr = [A.alloc(C), A.alloc(C)]
    hrow = [A.alloc(C), A.alloc(C)]
    st1 = {k: A.alloc(1) for k in ("m", "ss", "v", "r", "t0", "t1")}
    st1["sq"] = A.alloc(C)
    hfm = A.alloc(8 * 128)
    lfm = A.alloc(128)[0:16, :]
    ltm = A.alloc(16)
    nmx = A.alloc(1)
    zsum = A.alloc(1)
    ziv = A.alloc(1)
    probs = A.alloc(16)
    prT = A.alloc(128)[0:16, :]
    smax = A.alloc(8)[0:16, :]
    smaxT = A.alloc(16)[0:8, :]
    scol = A.alloc(16)
    scolT = A.alloc(128)[0:16, :]
    sgf = A.alloc(32 * 128)
    sst = A.alloc(128)
    hg = A.alloc(8 * 2048)
    gst = A.alloc(1024)
    fwq_sb = [A.alloc(8 * FDH), A.alloc(8 * FDH)]
    fwkv_sb = [A.alloc(8 * 2 * FDH), A.alloc(8 * 2 * FDH)]
    fwg_sb = A.alloc(8 * FH)
    qg = A.alloc(JPC)
    kg = A.alloc(JPC)
    kvst = A.alloc(JPC)
    dotst = A.alloc(JPC)[0:1, :]

    tk_fwg = s.dma("sync", _sview(fwg_sb, 8, FH), _wview(io["fwg"], 8, 128))
    hg_view = _r(hg, "p (c g j) -> p c g j", c=8, g=8, j=JPC)
    guards = {}
    p1_bank = {}

    def p1_tile(t, pass2):
        buf = t % 2
        if not pass2:
            s.waits("sync", [guards.get(("xr", buf))])
            tk_x = s.dma("sync", xr[buf], io["xf"][t * 128:(t + 1) * 128, :])
            s.waits("vector", [tk_x, guards.get(("hrow", buf))])
            tk_h = layernorm(xr[buf], hrow[buf], st1, C, eps_ln)
            guards[("xr", buf)] = tk_h
            s.wait("gpsimd", tk_h)
            s.dma("gpsimd", h_scr[t * 128:(t + 1) * 128, :], hrow[buf], tok=True)
            s.wait("tensor", tk_h)
            for half in range(2):
                pb = psT if half == 0 else psB
                s.waits("tensor", [p1_bank.get(id(pb))])
                for c4 in range(4):
                    ch = half * 4 + c4
                    tt = transpose(pb[:, c4 * 128:(c4 + 1) * 128],
                                   hrow[buf][:, ch * 128:(ch + 1) * 128],
                                   tok=(c4 == 3))
                s.wait("vector", tt)
                tv = s.emit("vector", lambda e, half=half, pb=pb: e.tensor_copy(
                    hfm[:, half * 512:(half + 1) * 512], pb[:, 0:512]), tok=True)
                p1_bank[id(pb)] = tv
            s.waits("tensor", [tk_fwg, p1_bank.get(id(psT)), p1_bank.get(id(psB)),
                               p1_bank.get(id(psA))])
            for ch in range(8):
                s.emit("tensor", lambda e, ch=ch: e.matmul(
                    psA[0:16, 0:128], fwg_sb[:, ch * 16:(ch + 1) * 16],
                    hfm[:, ch * 128:(ch + 1) * 128], start=(ch == 0), stop=(ch == 7)),
                    tok=(ch == 7))
            tmm = s.tok("tensor")
            s.wait("vector", tmm)
            tv = s.emit("vector", lambda e: e.tensor_copy(lfm, psA[0:16, 0:128]), tok=True)
            s.wait("tensor", tv)
            tt = transpose(psT[0:128, 0:16], lfm, tok=True)
            s.wait("vector", tt)
            s.emit("vector", lambda e: e.tensor_copy(ltm, psT[:, 0:16]))
            gapv()
            s.emit("vector", lambda e: e.reduce_max(nmx, ltm, axis=AX.X, negate=True))
            gapv()
            tv2 = s.emit("vector", lambda e: e.tensor_copy(gap1[:, 0:4], gap1[:, 4:8]),
                         tok=True)
            s.wait("scalar", tv2)
            ta = s.emit("scalar", lambda e: e.activation(out=probs, in_=ltm, func=AF.Exp,
                                                         bias=nmx, accum_out=zsum),
                        tok=True)
            s.wait("vector", ta)
            s.emit("vector", lambda e: e.reciprocal(ziv, zsum))
            gapv()
            tv = s.emit("vector", lambda e: e.tensor_scalar(probs, probs, ziv, None,
                                                            ALU.mult), tok=True)
            s.wait("tensor", tv)
            tt = transpose(psT[0:16, 0:128], probs, tok=True)
            s.wait("vector", tt)
            tv = s.emit("vector", lambda e: e.tensor_copy(prT, psT[0:16, 0:128]),
                        tok=True)
            gapv()
            prv = _r(prT, "p (j i) -> p j i", j=8, i=16)
            tv2 = s.emit("vector", lambda e, prv=prv: e.reduce_max(smax, prv, axis=AX.X),
                         tok=True)
            s.wait("tensor", tv2)
            tt = transpose(psB[0:8, 0:16], smax, tok=True)
            s.wait("vector", tt)
            tv = s.emit("vector", lambda e: e.tensor_copy(smaxT, psB[0:8, 0:16]),
                        tok=True)
            s.wait("tensor", tv)
            tmm = s.emit("tensor", lambda e: e.matmul(psB[:, 0:16], repj_sb, smaxT,
                                                      start=True, stop=True), tok=True)
            s.wait("vector", tmm)
            tv = s.emit("vector", lambda e: e.tensor_tensor(scol, probs, psB[:, 0:16],
                                                            ALU.is_equal), tok=True)
            s.wait("tensor", tv)
            tt = transpose(psB[0:16, 0:128], scol, tok=True)
            s.wait("vector", tt)
            tv = s.emit("vector", lambda e: e.tensor_copy(scolT, psB[0:16, 0:128]),
                        tok=True)
            s.wait("tensor", tv)
            tmm = s.emit("tensor", lambda e: e.matmul(psB[:, 0:128], reg_sb, scolT,
                                                      start=True, stop=True), tok=True)
            s.wait("vector", tmm)
            tv = s.emit("vector", lambda e: e.tensor_tensor(sst, psB[:, 0:128], mj_sb, ALU.mult),
                        tok=True)
            s.wait("gpsimd", tv)
            s.dma("gpsimd", ssc_scr[t], sst, tok=True)
            s.wait("tensor", tv)
            tt = transpose(psB[:, 0:128], sst, tok=True)
            s.wait("vector", tt)
            tsg = s.emit("vector", lambda e, t=t: e.tensor_copy(
                sgf[:, t * 128:(t + 1) * 128], psB[:, 0:128]), tok=True)
            s.wait("tensor", tsg)
            hf0 = 0
        else:
            s.waits("sync", [guards.get(("hrow", buf))])
            tk_x = s.dma("sync", hrow[buf], h_scr[t * 128:(t + 1) * 128, :])
            s.wait("tensor", tk_x)
            hf0 = 1
        s.waits("tensor", [p1_bank.get(id(psA)), p1_bank.get(id(psB))])
        for cf in range(2):
            s.emit("tensor", lambda e, t=t, cf=cf, buf=buf: e.matmul(
                psA if cf == 0 else psB,
                sgf[:, t * 128:(t + 1) * 128],
                hrow[buf][:, cf * 512:(cf + 1) * 512], start=True, stop=True),
                tok=(cf == 1))
        tmm = s.tok("tensor")
        guards[("hrow", buf)] = tmm
        s.wait("vector", tmm)
        s.waits("vector", [p1_bank.get(id(psB))])
        s.emit("vector", lambda e: e.tensor_copy(gst[:, 0:512], psA))
        tv = s.emit("vector", lambda e: e.tensor_copy(gst[:, 512:1024], psB),
                    tok=True)
        p1_bank[id(psA)] = tv
        s.wait("tensor", tv)
        for half in range(2):
            pb = psT if half == 0 else psB
            s.waits("tensor", [p1_bank.get(id(pb))])
            for c4 in range(4):
                ch = half * 4 + c4
                tt = transpose(pb[:, c4 * 128:(c4 + 1) * 128],
                               gst[:, ch * 128:(ch + 1) * 128], tok=(c4 == 3))
            s.wait("vector", tt)
            srcv = _r(pb, "p (c j g) -> p c g j", c=4, j=8, g=16)[
                :, :, hf0 * 8:hf0 * 8 + 8, :]
            dst = hg_view[:, half * 4:(half + 1) * 4, :, 8 * t: 8 * t + 8]
            tv = s.emit("vector", lambda e, srcv=srcv, dst=dst: e.tensor_copy(dst, srcv),
                        tok=True)
            p1_bank[id(pb)] = tv

    def p1_per_g(half):
        prev_vdma = None
        prev_ddma = None
        prev_dot = None
        for gl in range(8):
            g = 8 * half + gl
            pb = gl % 2
            s.waits("sync", [guards.get(("fw", pb))])
            tkw1 = s.dma("sync", _sview(fwq_sb[pb], 8, FDH),
                         _wview(io["fwq"][g], 8, 128))
            tkw2 = s.dma("sync", _sview(fwkv_sb[pb], 8, 2 * FDH),
                         _wview(io["fwkv"][g], 8, 128))
            s.waits("tensor", [tkw1, tkw2])
            for ch in range(8):
                s.emit("tensor", lambda e, ch=ch, pb=pb, gl=gl: e.matmul(
                    psA[0:64, 0:256], fwq_sb[pb][:, ch * 64:(ch + 1) * 64],
                    hg_view[:, ch, gl, :], start=(ch == 0), stop=(ch == 7)),
                    tok=(ch == 7))
            tq = s.tok("tensor")
            for ch in range(8):
                s.emit("tensor", lambda e, ch=ch, pb=pb, gl=gl: e.matmul(
                    psB[:, 0:256], fwkv_sb[pb][:, ch * 128:(ch + 1) * 128],
                    hg_view[:, ch, gl, :], start=(ch == 0), stop=(ch == 7)),
                    tok=(ch == 7))
            tkv = s.tok("tensor")
            guards[("fw", pb)] = tkv
            s.waits("scalar", [tq, prev_dot])
            ta1 = s.emit("scalar", lambda e, g=g: e.activation(
                out=qg[0:64, :], in_=psA[0:64, 0:256], func=AF.Identity,
                bias=fbqT_sb[0:64, g:g + 1]), tok=True)
            s.waits("scalar", [tkv, prev_vdma])
            ta2 = s.emit("scalar", lambda e, g=g: e.activation(
                out=kg[0:64, :], in_=psB[0:64, 0:256], func=AF.Identity,
                bias=fbkvT_sb[0:64, g:g + 1]), tok=True)
            ta3 = s.emit("scalar", lambda e, g=g: e.activation(
                out=kvst[64:128, :], in_=psB[64:128, 0:256], func=AF.Identity,
                bias=fbkvT_sb[64:128, g:g + 1]), tok=True)
            s.wait("gpsimd", ta3)
            prev_vdma = s.dma("gpsimd", vall[0:64, g * 256:(g + 1) * 256],
                              kvst[64:128, :], tok=True)
            s.waits("vector", [ta1, ta2])
            tv = s.emit("vector", lambda e: e.tensor_tensor(kvst[0:64, :], qg[0:64, :],
                                                            kg[0:64, :], ALU.mult),
                        tok=True)
            s.wait("tensor", tv)
            tmm = s.emit("tensor", lambda e: e.matmul(
                psT[0:1, 0:256], o128x1[0:64, :], kvst[0:64, :], start=True, stop=True),
                tok=True)
            s.waits("vector", [tmm, prev_ddma])
            prev_dot = s.emit("vector", lambda e: e.tensor_copy(dotst[:, 0:256],
                                                                psT[0:1, 0:256]), tok=True)
            s.wait("gpsimd", prev_dot)
            prev_ddma = s.dma("gpsimd", cc_att_in.ap()[:, g * 256:(g + 1) * 256],
                              dotst[:, 0:256], tok=True)

    tk_ccatt = None
    if upto in ("p1", "p2", "all"):
        for t in range(32):
            p1_tile(t, pass2=False)
        p1_per_g(0)
        s.wait("sync", s.tok("tensor"))
        for t in range(32):
            p1_tile(t, pass2=True)
        p1_per_g(1)
        s.wait("gpsimd", s.dtok("gpsimd"))
        tk_ccatt = s.collective("AllGather", ALU.bypass,
                                [cc_att_in.ap().opt()], [cc_att_out.ap().opt()])
        s.dma("gpsimd", attL_p,
              _r(cc_att_in.ap()[0].unsqueeze(0), "o (g j) -> (o g) j", g=FH, j=JPC),
              tok=True)
        if tap("hg", (128, 8 * 2048)) is not None:
            tv = s.tok("vector")
            s.wait("gpsimd", tv)
            s.dma("gpsimd", tap_aps["hg"], hg, tok=True)
        if tap("vall", (128, FH * JPC)) is not None:
            s.dma("gpsimd", tap_aps["vall"], vall, tok=True)
        if tap("attin", (1, FH * JPC)) is not None:
            s.dma("gpsimd", tap_aps["attin"], cc_att_in.ap(), tok=True)
        if tap("hscr", (FROWS, C)) is not None:
            s.dma("gpsimd", tap_aps["hscr"], h_scr, tok=True)

    A.pop()

    A.push()
    st2 = {k: A.alloc(1) for k in ("m", "ss", "v", "r", "t0", "t1")}
    st2["sq"] = A.alloc(C)
    kvc = A.alloc(2 * T)
    qc = A.alloc(4 * T)
    A.push()
    sq2 = A.alloc(512)
    xr2 = [A.alloc(C), A.alloc(C)]
    xn2 = [A.alloc(C), A.alloc(C)]
    xnf = A.alloc(8 * 512)
    wkv_sb = A.alloc(8 * KVR)
    wq_sb = A.alloc(8 * QR)
    wro_sb = A.alloc(8 * QR)
    ro_raw = A.alloc(512)
    ro_out = A.alloc(512)
    rms_s = A.alloc(512)[0:1, :]
    rms_r = A.alloc(512)[0:1, :]
    rms_t0 = A.alloc(512)[0:1, :]
    rms_t1 = A.alloc(512)[0:1, :]
    A.pop()

    rope_state = {"prev": None}

    def rope_block(dst_scr, oc, g4, psrc, bank_ev):
        tv = s.emit("vector", lambda e, psrc=psrc: e.tensor_copy(ro_raw, psrc), tok=True)
        bank_ev[id(psrc)] = tv
        s.wait("tensor", tv)
        trot = s.emit("tensor", lambda e: e.matmul(psB, pmat, ro_raw,
                                                   start=True, stop=True), tok=True)
        tsl = slice(g4 * 512, (g4 + 1) * 512)
        s.waits("vector", [rope_state["prev"]])
        s.emit("vector", lambda e, tsl=tsl: e.tensor_tensor(
            ro_out, ro_raw, coss[:, tsl], ALU.mult))
        s.wait("vector", trot)
        s.emit("vector", lambda e, tsl=tsl: e.tensor_tensor(
            ro_raw, psB, sins[:, tsl], ALU.mult))
        gapv()
        tv = s.emit("vector", lambda e: e.tensor_tensor(ro_out, ro_out, ro_raw,
                                                        ALU.add), tok=True)
        s.wait("gpsimd", tv)
        tk = s.dma("gpsimd", dst_scr[oc * 128:(oc + 1) * 128, tsl], ro_out, tok=True)
        rope_state["prev"] = tk
        return tk

    g2 = {}
    bank_ev = {}

    def a_phase(b):
        s.wait("sync", s.tok("tensor"))
        s.wait("sync", s.dtok("gpsimd"))
        s.wait("sync", s.tok("vector"))
        tw1 = s.dma("sync", _sview(wkv_sb, 8, KVR), _wview(io["wkv"], 8, 128))
        tw2 = s.dma("sync", _sview(wq_sb, 8, QR), _wview(io["wq"], 8, 128))
        tw3 = s.dma("sync", _sview(wro_sb, 8, QR), _wview(io["wkrope"], 8, 128))
        rope_toks = []
        for g4 in range(4):
            for sub in range(4):
                buf = sub % 2
                s.waits("sync", [g2.get(("xr2", buf))])
                tk_x = s.dma("sync", xr2[buf],
                             io["xa"][b, g4 * 512 + sub * 128: g4 * 512 + (sub + 1) * 128, :])
                s.waits("vector", [tk_x, g2.get(("xn2", buf))])
                tk_n = layernorm(xr2[buf], xn2[buf], st2, C, eps_ln)
                g2[("xr2", buf)] = tk_n
                s.wait("tensor", tk_n)
                xnf_v = _r(xnf, "p (c t) -> p c t", c=8, t=512)
                for half in range(2):
                    pb = psT if half == 0 else psB
                    s.waits("tensor", [bank_ev.get(id(pb))])
                    for c4 in range(4):
                        ch = half * 4 + c4
                        tt = transpose(pb[:, c4 * 128:(c4 + 1) * 128],
                                       xn2[buf][:, ch * 128:(ch + 1) * 128],
                                       tok=(c4 == 3))
                    if half == 1:
                        g2[("xn2", buf)] = tt
                    s.wait("vector", tt)
                    tv = s.emit("vector", lambda e, half=half, sub=sub, pb=pb,
                                xnf_v=xnf_v: e.tensor_copy(
                        xnf_v[:, half * 4:(half + 1) * 4, sub * 128:sub * 128 + 128],
                        _r(pb, "p (c t) -> p c t", c=4, t=128)), tok=True)
                    bank_ev[id(pb)] = tv
            s.waits("tensor", [tw1, tw2, tw3, bank_ev.get(id(psT)),
                               bank_ev.get(id(psB))])

            def proj_into(dst, w_sb, noc, wstride):
                for oc in range(noc):
                    pb = psA if oc % 2 == 0 else psT
                    s.waits("tensor", [bank_ev.get(id(pb))])
                    for ch in range(8):
                        s.emit("tensor", lambda e, oc=oc, ch=ch, pb=pb, w_sb=w_sb,
                               wstride=wstride: e.matmul(
                            pb, w_sb[:, ch * wstride + oc * 128: ch * wstride + oc * 128 + 128],
                            xnf[:, ch * 512:(ch + 1) * 512],
                            start=(ch == 0), stop=(ch == 7)), tok=(ch == 7))
                    tmm = s.tok("tensor")
                    s.wait("vector", tmm)
                    tv = s.emit("vector", lambda e, dst=dst, oc=oc, pb=pb, g4=g4:
                                e.tensor_copy(
                                    dst[:, oc * T + g4 * 512: oc * T + g4 * 512 + 512], pb),
                                tok=True)
                    bank_ev[id(pb)] = tv

            proj_into(kvc, wkv_sb, 2, KVR)
            proj_into(qc, wq_sb, 4, QR)
            for oc in range(4):
                s.waits("tensor", [bank_ev.get(id(psA))])
                for ch in range(8):
                    s.emit("tensor", lambda e, oc=oc, ch=ch: e.matmul(
                        psA, wro_sb[:, ch * QR + oc * 128: ch * QR + oc * 128 + 128],
                        xnf[:, ch * 512:(ch + 1) * 512],
                        start=(ch == 0), stop=(ch == 7)), tok=(ch == 7))
                s.wait("vector", s.tok("tensor"))
                rope_toks.append(rope_block(kro_scr, oc, g4, psA, bank_ev))
        s.wait("sync", s.tok("tensor"))
        tw4 = s.dma("sync", _sview(wro_sb[:, 0:4 * QR], 4, QR),
                    _wview(io["wqrope"], 4, 128))
        s.waits("tensor", [tw4])
        for g4 in range(4):
            for oc in range(4):
                s.waits("tensor", [bank_ev.get(id(psA))])
                for ch in range(4):
                    s.emit("tensor", lambda e, oc=oc, ch=ch, g4=g4: e.matmul(
                        psA, wro_sb[:, ch * QR + oc * 128: ch * QR + oc * 128 + 128],
                        qc[:, ch * T + g4 * 512: ch * T + g4 * 512 + 512],
                        start=(ch == 0), stop=(ch == 3)), tok=(ch == 3))
                s.wait("vector", s.tok("tensor"))
                rope_toks.append(rope_block(qro_scr, oc, g4, psA, bank_ev))
        for g4 in range(4):
            tsl = slice(g4 * 512, (g4 + 1) * 512)
            for ch in range(2):
                tvq = s.emit("vector", lambda e, ch=ch, tsl=tsl: e.tensor_tensor(
                    sq2, kvc[:, ch * T:][:, tsl], kvc[:, ch * T:][:, tsl], ALU.mult),
                    tok=True)
                s.wait("tensor", tvq)
                tms = s.emit("tensor", lambda e, ch=ch: e.matmul(
                    psT[0:1, 0:512], o128x1, sq2, start=(ch == 0), stop=(ch == 1)),
                    tok=True)
                s.wait("vector", tms)
            s.wait("vector", tms)
            tvv = s.emit("vector", lambda e: e.tensor_copy(rms_s, psT[0:1, 0:512]),
                         tok=True)
            s.wait("gpsimd", tvv)
            s.emit("gpsimd", lambda e: e.tensor_scalar(rms_s, rms_s,
                                                       1.0 / KVR, None, ALU.mult))
            tg = s.emit("gpsimd", lambda e: e.tensor_scalar(rms_s, rms_s,
                                                            eps_rms[0:1, :], None,
                                                            ALU.add), tok=True)
            tn = rsqrt_chain(rms_s, rms_r, rms_t0, rms_t1, tg)
            s.wait("tensor", tn)
            tb = s.emit("tensor", lambda e: e.matmul(psB, o1x128, rms_r,
                                                     start=True, stop=True), tok=True)
            s.wait("vector", tb)
            tvl = None
            for ch in range(2):
                tvl = s.emit("vector", lambda e, ch=ch, tsl=tsl: e.scalar_tensor_tensor(
                    kvc[:, ch * T:][:, tsl], kvc[:, ch * T:][:, tsl],
                    gkv_sb[:, ch:ch + 1], psB, ALU.mult, ALU.mult), tok=True)
            s.wait("tensor", tvl)
        return rope_toks

    A.push()
    Ke = A.alloc(T)[0:96, :]
    Ko = A.alloc(T)[0:96, :]
    Qe = A.alloc(T)[0:96, :]
    Qo = A.alloc(T)[0:96, :]
    kdst = A.alloc(512)
    Ve = A.alloc(16 * 65)
    Vo = A.alloc(16 * 65)
    PT = A.alloc(2 * 1024)
    wkd_sb = A.alloc(2 * 128)
    wvd_sb = A.alloc(2 * 128)
    wqd_sb = A.alloc(4 * 128)
    oa_sb = A.alloc(T)[0:64, :]
    avf = A.alloc(T)
    zpack = A.alloc(512)
    zrow = zpack
    zrow0 = zpack
    vstage = A.alloc(512)
    zx = vstage[0:64, :]
    head_state = {"oadma": None}
    norm_state = {"dma": None, "mult": None, "bc": None}

    def attn_head(b, h, Kb, Qb, Vb):
        first_oa = [True]
        pv_free = [head_state.get("pv_free")]
        pending = []

        def flush_norm():
            if not pending:
                return
            sq = pending.pop(0)
            s.waits("tensor", [norm_state["dma"], norm_state["mult"],
                               bank_ev.get(id(psB))])
            tbc = s.emit("tensor", lambda e: e.matmul(psB[0:64, :], o1x64,
                                                      zrow0[0:1, :],
                                                      start=True, stop=True), tok=True)
            s.wait("vector", tbc)
            tzx = s.emit("vector", lambda e: e.tensor_copy(zx, psB[0:64, :]), tok=True)
            bank_ev[id(psB)] = tzx
            tmu = s.emit("vector", lambda e, sq=sq: e.scalar_tensor_tensor(
                oa_sb[0:64, sq * 512:(sq + 1) * 512],
                avf[0:64, sq * 512:(sq + 1) * 512], 1.0, zx,
                ALU.mult, ALU.mult), tok=True)
            norm_state["mult"] = tmu
            norm_state["bc"] = tbc

        for sq4 in range(4):
            nk = 4 * sq4 + 4
            exp_toks = {}
            pv_state = {0: None, 1: None}

            def emit_pv(k0, nk=nk):
                s.wait("tensor", exp_toks[k0])
                if k0 == 0:
                    s.waits("tensor", [pv_free[0]])
                grp = (k0 // 2) % 2
                for j in range(2):
                    kj = k0 + j
                    s.emit("tensor", lambda e, kj=kj, grp=grp, nk=nk: e.matmul(
                        psPV[0:65, :],
                        Vb[:, kj * 65: kj * 65 + 65],
                        PT[:, grp * 1024 + (kj % 2) * 512: grp * 1024 + (kj % 2) * 512 + 512],
                        start=(kj == 0), stop=(kj == nk - 1)),
                        tok=(j == 1))
                pv_state[grp] = s.tok("tensor")

            for kk in range(nk):
                slot = kk % 4
                if kk == 3:
                    flush_norm()
                if kk >= 2 and kk % 2 == 0:
                    emit_pv(kk - 2)
                tmm = s.emit("tensor", lambda e, kk=kk, sq4=sq4, slot=slot, Kb=Kb, Qb=Qb:
                             e.matmul(
                    psS[:, slot * 512:(slot + 1) * 512], Kb[:, kk * 128:(kk + 1) * 128],
                    Qb[:, sq4 * 512:(sq4 + 1) * 512], start=True, stop=True), tok=True)
                d = 128 * kk - 512 * sq4
                if d >= 0:
                    s.wait("vector", tmm)
                    tmm = s.emit("vector", lambda e, slot=slot, d=d: e.tensor_tensor(
                        psS[:, slot * 512:(slot + 1) * 512],
                        psS[:, slot * 512:(slot + 1) * 512],
                        mstrip[:, 384 - d: 896 - d], ALU.add), tok=True)
                if kk % 2 == 1:
                    grp2 = (kk // 2) % 2
                    s.wait("scalar", tmm)
                    s.waits("scalar", [pv_state[grp2]])
                    exp_toks[kk - 1] = s.emit("scalar", lambda e, kk=kk, grp2=grp2:
                                              e.activation(
                        out=PT[:, grp2 * 1024:(grp2 + 1) * 1024],
                        in_=psS[:, ((kk - 1) % 4) * 512: ((kk - 1) % 4) * 512 + 1024],
                        func=AF.Exp, scale=SC_ATT), tok=True)
            emit_pv(nk - 2)
            tpv = pv_state[((nk - 2) // 2) % 2]
            s.wait("vector", tpv)
            if first_oa[0]:
                s.waits("vector", [head_state["oadma"]])
                first_oa[0] = False
            tev = s.emit("vector", lambda e, sq4=sq4: e.tensor_copy(
                avf[0:65, sq4 * 512:(sq4 + 1) * 512], psPV[0:65, :]), tok=True)
            pv_free[0] = tev
            s.waits("vector", [norm_state["dma"]])
            gapv()
            trc = s.emit("vector", lambda e, sq4=sq4: e.reciprocal(
                zrow[64:65, :], avf[64:65, sq4 * 512:(sq4 + 1) * 512]), tok=True)
            s.waits("gpsimd", [trc, norm_state["bc"]])
            norm_state["dma"] = s.dma("gpsimd", zrow0[0:1, :], zrow[64:65, :], tok=True)
            pending.append(sq4)
        flush_norm()
        head_state["pv_free"] = pv_free[0]
        tvf = s.tok("vector")
        s.wait("gpsimd", tvf)
        head_state["oadma"] = s.dma("gpsimd", oa_scr[h * 64:(h + 1) * 64, :],
                                    oa_sb[0:64, :], tok=True)

    def head_pair(b, p):
        s.waits("sync", [head_state.get("dec_tok")])
        twk = s.dma("sync", _sview(wkd_sb, 2, 128), _wview(io["wkdec"][:, p * 128:(p + 1) * 128], 2, 128))
        twv = s.dma("sync", _sview(wvd_sb, 2, 128), _wview(io["wvdec"][:, p * 128:(p + 1) * 128], 2, 128))
        twq = s.dma("sync", _sview(wqd_sb, 4, 128), _wview(io["wqdec"][:, p * 128:(p + 1) * 128], 4, 128))
        s.waits("tensor", [twk, twv, twq])
        shift_toks = []
        prev_kdma = g2.get(("rst",))
        for tt4 in range(4):
            tsl = slice(tt4 * 512, (tt4 + 1) * 512)
            for ch in range(2):
                s.emit("tensor", lambda e, ch=ch, tsl=tsl: e.matmul(
                    psA, wkd_sb[:, ch * 128:(ch + 1) * 128], kvc[:, ch * T:][:, tsl],
                    start=(ch == 0), stop=(ch == 1)), tok=(ch == 1))
            tmm = s.tok("tensor")
            s.wait("vector", tmm)
            s.emit("vector", lambda e, tsl=tsl: e.tensor_copy(Ke[0:64, tsl], psA[0:64, :]))
            s.waits("vector", [prev_kdma])
            tv = s.emit("vector", lambda e: e.tensor_copy(kdst, psA), tok=True)
            s.wait("gpsimd", tv)
            prev_kdma = s.dma("gpsimd", Ko[0:64, tsl], kdst[64:128, :], tok=True)
            shift_toks.append(prev_kdma)
            s.wait("tensor", tv)
            for ch in range(4):
                s.emit("tensor", lambda e, ch=ch, tsl=tsl: e.matmul(
                    psA, wqd_sb[:, ch * 128:(ch + 1) * 128], qc[:, ch * T:][:, tsl],
                    start=(ch == 0), stop=(ch == 3)), tok=(ch == 3))
            tmm = s.tok("tensor")
            s.wait("vector", tmm)
            s.emit("vector", lambda e, tsl=tsl: e.tensor_copy(Qe[0:64, tsl], psA[0:64, :]))
            s.waits("vector", [prev_kdma])
            tv = s.emit("vector", lambda e: e.tensor_copy(kdst, psA), tok=True)
            s.wait("gpsimd", tv)
            prev_kdma = s.dma("gpsimd", Qo[0:64, tsl], kdst[64:128, :], tok=True)
            shift_toks.append(prev_kdma)
            s.wait("tensor", tv)
            for ch in range(2):
                s.emit("tensor", lambda e, ch=ch, tsl=tsl: e.matmul(
                    psA, wvd_sb[:, ch * 128:(ch + 1) * 128], kvc[:, ch * T:][:, tsl],
                    start=(ch == 0), stop=(ch == 1)), tok=(ch == 1))
            tmm = s.tok("tensor")
            s.wait("vector", tmm)
            tv = s.emit("vector", lambda e: e.tensor_copy(vstage, psA), tok=True)
            s.wait("tensor", tv)
            for k4 in range(4):
                kk = tt4 * 4 + k4
                ttk = transpose(psT[0:128, 0:128], vstage[:, k4 * 128:(k4 + 1) * 128],
                                tok=True)
                s.wait("vector", ttk)
                s.emit("vector", lambda e, kk=kk: e.tensor_copy(
                    Ve[:, kk * 65: kk * 65 + 64], psT[:, 0:64]))
                tv = s.emit("vector", lambda e, kk=kk: e.tensor_copy(
                    Vo[:, kk * 65: kk * 65 + 64], psT[:, 64:128]), tok=True)
                s.wait("tensor", tv)
        head_state["dec_tok"] = s.tok("tensor")
        s.wait("gpsimd", s.tok("tensor"))
        for h, Kb2, Qb2 in ((2 * p, Ke, Qe), (2 * p + 1, Ko, Qo)):
            r0 = (h // 4) * 128 + (h % 4) * 32
            shift_toks.append(s.dma("gpsimd", Kb2[64:96, :], kro_scr[r0:r0 + 32, :],
                                    tok=True))
            shift_toks.append(s.dma("gpsimd", Qb2[64:96, :], qro_scr[r0:r0 + 32, :],
                                    tok=True))
        s.emit("vector", lambda e: e.memset(
            _r(Ve, "p (k w) -> p k w", k=16, w=65)[:, :, 64:65], 1.0))
        tvm = s.emit("vector", lambda e: e.memset(
            _r(Vo, "p (k w) -> p k w", k=16, w=65)[:, :, 64:65], 1.0), tok=True)
        s.waits("tensor", shift_toks + [tvm])
        if p == 0 and b == 1 and "Ke" in taps:
            s.waits("gpsimd", shift_toks + [tvm])
            s.dma("gpsimd", tap_aps["Ke"], Ke, tok=True)
            s.dma("gpsimd", tap_aps["Qe"], Qe, tok=True)
            s.dma("gpsimd", tap_aps["Ve"], Ve, tok=True)
        attn_head(b, 2 * p, Ke, Qe, Ve)
        attn_head(b, 2 * p + 1, Ko, Qo, Vo)

    wo_sb = [A.alloc(512), A.alloc(512), PT[:, 0:512], PT[:, 512:1024]]
    xres = [A.alloc(C), A.alloc(C)]
    oaT = [A.alloc(128), A.alloc(128), PT[:, 1024:1152], PT[:, 1152:1280]]
    rst = kdst

    def wo_phase(b):
        tk_oa = s.dtok("gpsimd")
        s.wait("sync", tk_oa)
        s.wait("sync", s.tok("tensor"))
        prev_rdma = None
        wg = {}
        for tc in range(16):
            bufx = tc % 2
            s.waits("sync", [wg.get(("xres", bufx))])
            tk_x = s.dma("sync", xres[bufx],
                         io["xa"][b, tc * 128:(tc + 1) * 128, :])
            for chalf in range(2):
                pb = psA if chalf == 0 else psB
                tmm = None
                for ch in range(8):
                    bufw = ch % 4
                    s.waits("sync", [wg.get(("wo", bufw))])
                    tk_w = s.dma("sync", wo_sb[bufw],
                                 io["wo"][ch * 128:(ch + 1) * 128,
                                          chalf * 512:(chalf + 1) * 512])
                    tk_o = s.dma("sync", oaT[bufw],
                                 oa_scr[ch * 128:(ch + 1) * 128,
                                        tc * 128:(tc + 1) * 128])
                    s.waits("tensor", [tk_w, tk_o])
                    tpe = s.emit("tensor", lambda e, pb=pb, bufw=bufw, ch=ch: e.matmul(
                        pb, oaT[bufw], wo_sb[bufw], start=(ch == 0), stop=(ch == 7)),
                        tok=True)
                    wg[("wo", bufw)] = tpe
                tmm = s.tok("tensor")
                s.waits("vector", [tmm, tk_x, prev_rdma])
                tv = s.emit("vector", lambda e, pb=pb, bufx=bufx, chalf=chalf:
                            e.tensor_tensor(rst, pb,
                                            xres[bufx][:, chalf * 512:(chalf + 1) * 512],
                                            ALU.add), tok=True)
                s.wait("tensor", tv)
                s.wait("gpsimd", tv)
                if chalf == 1:
                    wg[("xres", bufx)] = tv
                prev_rdma = s.dma(
                    "gpsimd",
                    cc_ra_in.ap()[b * T + tc * 128: b * T + (tc + 1) * 128,
                                  chalf * 512:(chalf + 1) * 512],
                    rst, tok=True)
                g2[("rst",)] = prev_rdma

    tap("Ke", (96, T)); tap("Qe", (96, T)); tap("Ve", (128, 16 * 65))
    tk_ccra = None
    if upto in ("p2", "all"):
        for b in range(BPC):
            rt = a_phase(b)
            s.waits("tensor", rt)
            for p in range(8):
                head_pair(b, p)
            wo_phase(b)
        s.wait("gpsimd", s.dtok("gpsimd"))
        tk_ccra = s.collective("AllGather", ALU.bypass,
                               [cc_ra_in.ap().opt()], [cc_ra_out.ap().opt()])
        if tap("kvc", (128, 2 * T)) is not None:
            tv = s.tok("vector")
            s.wait("gpsimd", tv)
            s.dma("gpsimd", tap_aps["kvc"], kvc, tok=True)
        if tap("oascr", (C, T)) is not None:
            s.dma("gpsimd", tap_aps["oascr"], oa_scr, tok=True)
        if tap("rain", (BPC * T, C)) is not None:
            s.wait("gpsimd", tk_ccra)
            s.dma("gpsimd", tap_aps["rain"], cc_ra_in.ap(), tok=True)
    A.pop()
    A.pop()

    A.push()
    attF = A.alloc(2048)[0:16, :]
    attL = A.alloc(256)[0:16, :]
    am = A.alloc(1)[0:16, :]
    z16 = A.alloc(1)[0:16, :]
    zi16 = A.alloc(1)[0:16, :]
    pmy = A.alloc(256)[0:16, :]
    pfl = A.alloc(FH * JPC)[0:1, :]
    pbc = A.alloc(FH * JPC)[0:64, :]
    av = A.alloc(FH * JPC)[0:64, :]
    fwo_sb = [A.alloc(C)[0:64, :], A.alloc(C)[0:64, :],
              A.alloc(C)[0:64, :], A.alloc(C)[0:64, :]]
    g3 = {}
    aost = A.alloc(1024)
    ssc_sb = [A.alloc(128), A.alloc(128), A.alloc(128), A.alloc(128)]
    aot_sb = [A.alloc(C), A.alloc(C), A.alloc(C), A.alloc(C)]
    hti = [A.alloc(C), A.alloc(C), A.alloc(C), A.alloc(C)]
    ysb = A.alloc(C)
    ynb = A.alloc(C)
    rat = [A.alloc(C), A.alloc(C)]
    st3 = {k: A.alloc(1) for k in ("m", "ss", "v", "r", "t0", "t1")}
    st3["sq"] = A.alloc(C)

    if upto == "all":
        s.wait("tensor", s.tok("vector"))
        s.wait("sync", s.dtok("gpsimd"))
        s.wait("sync", s.tok("vector"))
        s.wait("sync", s.tok("tensor"))
        s.wait("gpsimd", s.tok("vector"))
        s.wait("gpsimd", tk_ccatt)
        tk_af = s.dma("gpsimd", _r(attF, "p (r j) -> p r j", r=NCORE, j=JPC),
                      _r(cc_att_out.ap(), "r (g j) -> g r j", g=FH, j=JPC), tok=True)
        tk_al = None
        s.waits("vector", [tk_af])
        tv = s.emit("vector", lambda e: e.reduce_max(am, attF, axis=AX.X, negate=True),
                    tok=True)
        s.wait("gpsimd", tv)
        tg = s.emit("gpsimd", lambda e: e.tensor_scalar(am, am, SC_FFN, None, ALU.mult),
                    tok=True)
        s.waits("scalar", [tg])
        ta = s.emit("scalar", lambda e: e.activation(out=attF, in_=attF, func=AF.Exp,
                                                     scale=SC_FFN, bias=am,
                                                     accum_out=z16), tok=True)
        ta2 = s.emit("scalar", lambda e: e.activation(out=pmy, in_=attL_p, func=AF.Exp,
                                                      scale=SC_FFN, bias=am), tok=True)
        s.waits("vector", [ta, ta2])
        s.emit("vector", lambda e: e.reciprocal(zi16, z16))
        gapv()
        tv = s.emit("vector", lambda e: e.tensor_scalar(pmy, pmy, zi16, None, ALU.mult),
                    tok=True)
        s.wait("gpsimd", tv)
        tk_p1 = s.dma("gpsimd", p_scr, pmy, tok=True)
        s.wait("gpsimd", tk_p1)
        tk_p2 = s.dma("gpsimd", pfl, _r(p_scr, "g j -> (g j)").unsqueeze(0), tok=True)
        s.wait("tensor", tk_p2)
        for i8 in range(8):
            tmm = s.emit("tensor", lambda e, i8=i8: e.matmul(
                psB[0:64, :], o1x64, pfl[:, i8 * 512:(i8 + 1) * 512],
                start=True, stop=True), tok=True)
            s.wait("vector", tmm)
            tv = s.emit("vector", lambda e, i8=i8: e.tensor_copy(
                pbc[:, i8 * 512:(i8 + 1) * 512], psB[0:64, :]), tok=True)
            s.wait("tensor", tv)
        tvav = s.emit("vector", lambda e: e.tensor_tensor(av, vall[0:64, :], pbc,
                                                          ALU.mult), tok=True)
        prev_adma = None
        for g in range(FH):
            bufw = g % 4
            s.waits("sync", [g3.get(("fwo", bufw))])
            tk_w = s.dma("sync", fwo_sb[bufw], io["fwo"][g])
            s.waits("tensor", [tk_w, tvav])
            for jh in range(2):
                tmm = None
                for chalf in range(2):
                    pb = psA if chalf == 0 else psB
                    s.emit("tensor", lambda e, g=g, jh=jh, chalf=chalf, pb=pb, bufw=bufw:
                           e.matmul(pb,
                                    av[:, g * 256 + jh * 128: g * 256 + jh * 128 + 128],
                                    fwo_sb[bufw][:, chalf * 512:(chalf + 1) * 512],
                                    start=True, stop=True), tok=(chalf == 1))
                tmm = s.tok("tensor")
                g3[("fwo", bufw)] = tmm
                s.waits("vector", [tmm, prev_adma])
                tv2 = None
                for chalf in range(2):
                    pb = psA if chalf == 0 else psB
                    tv2 = s.emit("vector", lambda e, pb=pb, chalf=chalf: e.tensor_tensor(
                        aost[:, chalf * 512:(chalf + 1) * 512], pb,
                        bob[:, chalf * 512:(chalf + 1) * 512], ALU.add),
                        tok=(chalf == 1))
                s.wait("tensor", tv2)
                s.wait("scalar", tv2)
                dstv = _r(ao_scr, "(j g) c -> j g c", j=JPC, g=FH)[
                    jh * 128:(jh + 1) * 128, g:g + 1, :]
                prev_adma = s.dma("scalar", dstv, aost.unsqueeze(1), tok=True)
        tk_ao = s.dtok("scalar")

        s.wait("tensor", s.tok("vector"))
        s.wait("sync", tk_ao)
        s.wait("gpsimd", tk_ccra)
        prev_odma = None
        prev_yadd = None
        dbg_toks = {"y": None, "f": None}
        ra_flat = cc_ra_out.ap()
        for t in range(32):
            bt = t % 4
            s.waits("sync", [g3.get(("tile", bt))])
            tk_s = s.dma("sync", ssc_sb[bt], ssc_scr[t])
            tk_a = s.dma("sync", aot_sb[bt], ao_scr[t * 128:(t + 1) * 128, :])
            tk_h = s.dma("sync", hti[bt], h_scr[t * 128:(t + 1) * 128, :])
            rt = t % 2
            s.waits("gpsimd", [g3.get(("rat", rt))])
            tk_r = s.idma(rat[rt], ra_flat, ridx_sb[:, t:t + 1])
            s.waits("tensor", [tk_s, tk_a, prev_yadd])
            for chalf in range(2):
                pb = psA if chalf == 0 else psB
                s.emit("tensor", lambda e, bt=bt, chalf=chalf, pb=pb: e.matmul(
                    pb, ssc_sb[bt], aot_sb[bt][:, chalf * 512:(chalf + 1) * 512],
                    start=True, stop=True), tok=(chalf == 1))
            tmm = s.tok("tensor")
            s.waits("vector", [tmm, tk_h])
            tv = None
            s.waits("vector", [dbg_toks["y"]])
            for chalf in range(2):
                pb = psA if chalf == 0 else psB
                tv = s.emit("vector", lambda e, bt=bt, chalf=chalf, pb=pb: e.tensor_tensor(
                    ysb[:, chalf * 512:(chalf + 1) * 512], pb,
                    hti[bt][:, chalf * 512:(chalf + 1) * 512], ALU.add),
                    tok=(chalf == 1))
            s.wait("tensor", tv)
            prev_yadd = tv
            g3[("tile", bt)] = tv
            if "dbgy" in taps:
                s.wait("gpsimd", tv)
                dbg_toks["y"] = s.dma("gpsimd", dbg_y[t * 128:(t + 1) * 128, :], ysb,
                                      tok=True)
            s.waits("vector", [prev_odma, dbg_toks["f"]])
            tk_n = layernorm(ysb, ynb, st3, C, eps_ln)
            s.waits("vector", [tk_n, tk_r])
            s.emit("vector", lambda e: e.tensor_tensor(ynb, ynb, lngb, ALU.mult))
            tfa = s.emit("vector", lambda e: e.tensor_tensor(ynb, ynb, lnbb, ALU.add),
                         tok=True)
            if "dbgy" in taps:
                s.wait("gpsimd", tfa)
                dbg_toks["f"] = s.dma("gpsimd", dbg_f[t * 128:(t + 1) * 128, :], ynb,
                                      tok=True)
            tv = s.emit("vector", lambda e, rt=rt: e.tensor_tensor(ynb, ynb, rat[rt],
                                                                   ALU.add), tok=True)
            g3[("rat", rt)] = tv
            s.wait("scalar", tv)
            outT = _r(out, "i j c -> j i c")
            for jr in range(8):
                prev_odma = s.dma("scalar", outT[t * 8 + jr],
                                  ynb[jr * 16:(jr + 1) * 16, :], tok=True)
        if tap("pmy", (16, JPC)) is not None:
            tv = s.tok("vector")
            s.wait("gpsimd", tv)
            s.dma("gpsimd", tap_aps["pmy"], pmy, tok=True)
        if tap("av", (64, FH * JPC)) is not None:
            s.dma("gpsimd", tap_aps["av"], av, tok=True)
        if tap("aoscr", (FROWS, C)) is not None:
            s.dma("gpsimd", tap_aps["aoscr"], ao_scr, tok=True)
        if tap("attf", (16, 2048)) is not None:
            s.dma("gpsimd", tap_aps["attf"], attF, tok=True)
        if tap("stats", (16, 4)) is not None:
            s.dma("gpsimd", tap_aps["stats"][:, 0:1], am, tok=True,
                  allow_slow_non_contiguous=True)
            s.dma("gpsimd", tap_aps["stats"][:, 1:2], z16, tok=True,
                  allow_slow_non_contiguous=True)
            s.dma("gpsimd", tap_aps["stats"][:, 2:3], zi16, tok=True,
                  allow_slow_non_contiguous=True)
        if tap("pmyt", (16, JPC)) is not None:
            s.dma("gpsimd", tap_aps["pmyt"], pmy, tok=True)
        if tap("ssc", (32 * 128, 128)) is not None:
            s.dma("gpsimd", tap_aps["ssc"], _r(ssc_scr, "t p c -> (t p) c"), tok=True)
        if tap("dbgy", (FROWS, C)) is not None:
            s.dma("gpsimd", tap_aps["dbgy"], dbg_y, tok=True)
        if tap("dbgf", (FROWS, C)) is not None:
            s.dma("gpsimd", tap_aps["dbgf"], dbg_f, tok=True)
        s.wait("gpsimd", s.dtok("gpsimd"))
        s.wait("gpsimd", s.dtok("scalar"))
    A.pop()

    with nc.Block() as block:
        @block.sync
        def _(eng):
            for f in s.ops["sync"]:
                f(eng)
        @block.tensor
        def _(eng):
            for f in s.ops["tensor"]:
                f(eng)
        @block.vector
        def _(eng):
            for f in s.ops["vector"]:
                f(eng)
        @block.scalar
        def _(eng):
            for f in s.ops["scalar"]:
                f(eng)
        @block.gpsimd
        def _(eng):
            for f in s.ops["gpsimd"]:
                f(eng)

    return nc, sorted(tap_aps)


def make_in_maps(inputs):
    consts = host_constants()
    x = np.ascontiguousarray(inputs["L0_attn_x"], dtype=np.float32)
    in_maps = []
    for r in range(NCORE):
        m = {
            "xa": np.ascontiguousarray(x[2 * r:2 * r + 2]),
            "xf": np.ascontiguousarray(
                x[:, r * JPC:(r + 1) * JPC, :].transpose(1, 0, 2).reshape(FROWS, C)),
            "wkv": inputs["L0_attn_W_kv"], "gkv": inputs["L0_attn_g_kv"],
            "wkdec": inputs["L0_attn_W_kdec"], "wvdec": inputs["L0_attn_W_vdec"],
            "wq": inputs["L0_attn_W_q"], "wqdec": inputs["L0_attn_W_qdec"],
            "wkrope": inputs["L0_attn_W_krope"], "wqrope": inputs["L0_attn_W_qrope"],
            "wo": inputs["L0_attn_W_o"],
            "fwg": inputs["L0_ffn_Wg"], "fwq": inputs["L0_ffn_Wq"],
            "fbqT": np.ascontiguousarray(np.asarray(inputs["L0_ffn_bq"]).T),
            "fwkv": inputs["L0_ffn_Wkv"],
            "fbkvT": np.ascontiguousarray(np.asarray(inputs["L0_ffn_bkv"]).T),
            "fwo": inputs["L0_ffn_Wo"], "fbo": inputs["L0_ffn_bo"],
            "flng": inputs["L0_ffn_ln_g"], "flnb": inputs["L0_ffn_ln_b"],
            "ridx": host_ridx(r),
        }
        m.update(consts)
        in_maps.append({k: np.ascontiguousarray(np.asarray(v, dtype=np.uint32)
                                                if k == "ridx"
                                                else np.asarray(v, dtype=np.float32))
                        for k, v in m.items()})
    return in_maps


def kernel(**inputs):
    nc, _ = build_program("all")
    in_maps = make_in_maps(inputs)
    res = run_bass_kernel_spmd(nc, in_maps, core_ids=list(range(NCORE)))
    shards = [res.results[r]["out"] for r in range(NCORE)]
    return np.ascontiguousarray(np.concatenate(shards, axis=1))
